# version 02
# baseline (speedup 1.0000x reference)
import numpy as np
import ml_dtypes
import concourse.bass as bass
import concourse.mybir as mybir
from concourse.bass_utils import run_bass_kernel_spmd

F32 = mybir.dt.float32
BF16 = mybir.dt.bfloat16
I32 = mybir.dt.int32
AF = mybir.ActivationFunctionType
OP = mybir.AluOpType
AX = mybir.AxisListType

LAST_TRACE = {}
SET_SQRT = 3
SET_LNEXP = 6
SET_SILU = 18

D = 1024
NH = 4
DH = 256
HID = 3072


class _Stop(Exception):
    pass


def build(S=8192, G=4, phases=99):
    NB = S // 128
    S4 = S // 4
    KOWN = S4 // 128
    C = S // 128
    assert C <= 64 and NB <= 64 and KOWN <= 64
    NGRP = KOWN // G
    GW = G * 128
    SBW = min(256, S4)
    NSB = S4 // SBW
    NHT = HID // 128
    NS4 = SBW // 128
    WN4 = min(512, S4)
    NW4 = S4 // WN4
    inv127 = float(1.0 / 127.0)

    nc = bass.Bass("TRN2", target_bir_lowering=False)

    xfull = nc.dram_tensor("xfull", [S, D], F32, kind="ExternalInput").ap()
    xown = nc.dram_tensor("xown", [S4, D], F32, kind="ExternalInput").ap()
    wq_in = nc.dram_tensor("wq", [D, 128], F32, kind="ExternalInput").ap()
    wk_in = nc.dram_tensor("wk", [D, 128], F32, kind="ExternalInput").ap()
    wv_in = nc.dram_tensor("wv", [D, 128], F32, kind="ExternalInput").ap()
    wi_in = nc.dram_tensor("wi", [NH, 3 * D], F32, kind="ExternalInput").ap()
    wf_in = nc.dram_tensor("wf", [NH, 3 * D], F32, kind="ExternalInput").ap()
    bi_in = nc.dram_tensor("bi", [1, NH], F32, kind="ExternalInput").ap()
    bf_in = nc.dram_tensor("bf", [1, NH], F32, kind="ExternalInput").ap()
    lnw_in = nc.dram_tensor("lnw", [1, D], F32, kind="ExternalInput").ap()
    wgT_in = nc.dram_tensor("wgT", [D, HID], BF16, kind="ExternalInput").ap()
    wuT_in = nc.dram_tensor("wuT", [D, HID], BF16, kind="ExternalInput").ap()
    wdT_in = nc.dram_tensor("wdT", [HID, D], BF16, kind="ExternalInput").ap()
    pm_in = nc.dram_tensor("pm", [KOWN * 128, 64], F32, kind="ExternalInput").ap()
    idx_in = nc.dram_tensor("idx", [KOWN, 1], I32, kind="ExternalInput").ap()
    y_out = nc.dram_tensor("y", [S4, D], F32, kind="ExternalOutput").ap()

    xln_dram = nc.dram_tensor("xln_d", [S, D], BF16, kind="Internal").ap()
    xlno_dram = nc.dram_tensor("xlno_d", [S4, D], BF16, kind="Internal").ap()
    vT_dram = nc.dram_tensor("vT_d", [DH, S], BF16, kind="Internal").ap()
    ser_dram = nc.dram_tensor("ser_d", [8, S], F32, kind="Internal").ap()
    sero_dram = nc.dram_tensor("sero_d", [8, S4], F32, kind="Internal").ap()
    M_dram = nc.dram_tensor("M_d", [S + 1, 1], F32, kind="Internal").ap()
    CS_dram = nc.dram_tensor("CS_d", [S + 1, 1], F32, kind="Internal").ap()
    g_dram = nc.dram_tensor("g_d", [NB, 128], F32, kind="Internal").ap()
    tot_dram = nc.dram_tensor("tot_d", [128, 1], F32, kind="Internal").ap()
    row_dram = nc.dram_tensor("row_d", [128, 1], F32, kind="Internal").ap()
    neg_dram = nc.dram_tensor("neg_d", [1, 1], F32, kind="Internal").ap()
    ck_dram = nc.dram_tensor("ck_d", [KOWN, 1], F32, kind="Internal").ap()
    osum_dram = nc.dram_tensor("osum_d", [1, GW], F32, kind="Internal").ap()
    attn_dram = nc.dram_tensor("attn_d", [S4, D], BF16, kind="Internal").ap()
    xmid_dram = nc.dram_tensor("xmid_d", [S4, D], F32, kind="Internal").ap()
    xmln_dram = nc.dram_tensor("xmln_d", [S4, D], BF16, kind="Internal").ap()

    from contextlib import ExitStack
    cnt = {"L": 0, "Q": 0, "Gd": 0, "T": 0, "V": 0, "A": 0, "Gc": 0, "O": 0}
    tbl = {"cur": None}
    global LAST_TRACE
    LAST_TRACE = {}

    def _ev(eng, kind, key, val):
        LAST_TRACE.setdefault(id(eng), []).append((kind, key, val))

    try:
      with ExitStack() as outer:
        e = outer.enter_context
        sL = e(nc.semaphore("sL"))
        sQ = e(nc.semaphore("sQ"))
        sGd = e(nc.semaphore("sGd"))
        sT = e(nc.semaphore("sT"))
        sV = e(nc.semaphore("sV"))
        sA = e(nc.semaphore("sA"))
        sGc = e(nc.semaphore("sGc"))
        sO = e(nc.semaphore("sO"))
        SEM = {"L": sL, "Q": sQ, "Gd": sGd, "T": sT, "V": sV, "A": sA,
               "Gc": sGc, "O": sO}

        CT0 = e(nc.psum_tensor("CT0", [128, 512], F32))
        CT1 = e(nc.psum_tensor("CT1", [128, 512], F32))
        OUT0 = e(nc.psum_tensor("OUT0", [128, 512], F32))
        OUT1 = e(nc.psum_tensor("OUT1", [128, 512], F32))
        OSUM = e(nc.psum_tensor("OSUM", [1, 512], F32))
        TPB = e(nc.psum_tensor("TPB", [128, 512], BF16))
        TPF = e(nc.psum_tensor("TPF", [128, 512], F32))
        GPS = e(nc.psum_tensor("GPS", [8, 512], F32))

        wq_bf = e(nc.sbuf_tensor("wq_bf", [128, D], BF16))
        wk_bf = e(nc.sbuf_tensor("wk_bf", [128, D], BF16))
        wv_bf = e(nc.sbuf_tensor("wv_bf", [128, D], BF16))
        wg8 = e(nc.sbuf_tensor("wg8", [128, 64], BF16))
        gamma = e(nc.sbuf_tensor("gamma", [128, D], F32))
        idf = e(nc.sbuf_tensor("idf", [128, 128], F32))
        idb = e(nc.sbuf_tensor("idb", [128, 128], BF16))
        triu = e(nc.sbuf_tensor("triu", [128, 128], F32))
        onesb = e(nc.sbuf_tensor("onesb", [128, 1], BF16))
        onesf = e(nc.sbuf_tensor("onesf", [128, 128], F32))
        eps5 = e(nc.sbuf_tensor("eps5", [128, 1], F32))
        onev = e(nc.sbuf_tensor("onev", [128, 1], F32))
        negi = e(nc.sbuf_tensor("negi", [128, 1], F32))
        zero1 = e(nc.sbuf_tensor("zero1", [128, 1], F32))
        biasbc = e(nc.sbuf_tensor("biasbc", [128, 64], F32))

        stA = e(nc.sbuf_tensor("stA", [128, 2 * 64], F32))
        stB = e(nc.sbuf_tensor("stB", [128, 6 * 64], F32))
        ser_st = [e(nc.sbuf_tensor(f"ser_st{i}", [8, 512], F32)) for i in range(2)]
        scn = e(nc.sbuf_tensor("scn", [128, 8 * 64], F32))
        row128 = e(nc.sbuf_tensor("row128", [1, 256], F32))
        colA = e(nc.sbuf_tensor("colA", [128, 1], F32))
        colB = e(nc.sbuf_tensor("colB", [128, 1], F32))
        ckcol = e(nc.sbuf_tensor("ckcol", [KOWN, 1], F32))
        ckrow = e(nc.sbuf_tensor("ckrow", [1, 64], F32))
        gj_sb = e(nc.sbuf_tensor("gj_sb", [NB, 128], F32))
        g_tc = e(nc.sbuf_tensor("g_tc", [128, 64], F32))
        pm_sb = e(nc.sbuf_tensor("pm_sb", [128, KOWN * 64], F32))
        idx_sb = e(nc.sbuf_tensor("idx_sb", [KOWN, 1], I32))
        Mb_sb = e(nc.sbuf_tensor("Mb_sb", [KOWN, 1], F32))
        CSb_sb = e(nc.sbuf_tensor("CSb_sb", [KOWN, 1], F32))
        own = e(nc.sbuf_tensor("own", [KOWN, 12 * 128], F32))
        okt = e(nc.sbuf_tensor("okt", [128, 3 * 64], F32))
        Ahat = e(nc.sbuf_tensor("Ahat", [128, KOWN * 64], F32))
        osum_all = e(nc.sbuf_tensor("osum_all", [KOWN, 128], F32))
        osr = e(nc.sbuf_tensor("osr", [1, 512], F32))
        s128 = e(nc.sbuf_tensor("s128", [128, 8], F32))

        block = e(nc.Block())

        (L_IG, L_FG, L_LF, L_CS, L_G, L_M, L_B, L_ET, L_AO, L_T1, L_T2,
         L_T3) = [128 * i for i in range(12)]

        def table(s, idx):
            if tbl["cur"] != idx:
                s.add_instruction(mybir.InstLoadActFuncSet(
                    name=nc.get_next_instruction_name(),
                    act_func_set_id=idx, ins=[], outs=[]))
                for _ in range(4):
                    s.nop(cycle_cnt=2000)
                s.drain()
                tbl["cur"] = idx

        def fin(eng, key):
            eng.drain().then_inc(SEM[key], 1)
            cnt[key] += 1
            _ev(eng, "inc", key, 1)

        def dma(eng, key, out, in_):
            eng.dma_start(out=out, in_=in_).then_inc(SEM[key], 16)
            cnt[key] += 16
            _ev(eng, "inc", key, 16)

        def dmat(eng, key, out, in_):
            eng.dma_start_transpose(out=out, in_=in_).then_inc(SEM[key], 16)
            cnt[key] += 16
            _ev(eng, "inc", key, 16)

        def w(eng, *kv):
            for k, v in kv:
                if v > 0:
                    eng.wait_ge(SEM[k], v)
                    _ev(eng, "wait", k, v)

        def now(key):
            return cnt[key]

        def maybe_stop(n):
            if phases <= n:
                @block.sync
                def _(sy):
                    w(sy, ("V", now("V")), ("A", now("A")), ("T", now("T")),
                      ("Gd", now("Gd")), ("Gc", now("Gc")), ("Q", now("Q")),
                      ("L", now("L")))
                    for i in range(S4 // 128):
                        sy.dma_start(out=y_out[128 * i:128 * i + 128, :],
                                     in_=gamma[:]).then_inc(sO, 16)
                        cnt["O"] += 16
                        _ev(sy, "inc", "O", 16)
                    sy.wait_ge(sO, cnt["O"])
                    _ev(sy, "wait", "O", cnt["O"])
                raise _Stop

        @block.gpsimd
        def _(g):
            g.memset(onesb[:], 1.0)
            g.memset(onesf[:], 1.0)
            g.memset(eps5[:], 1e-5)
            g.memset(onev[:], 1.0)
            g.memset(negi[:], -1e30)
            g.memset(zero1[:], 0.0)
            g.memset(stA[:], 0.0)
            g.memset(stB[:], 0.0)
            g.memset(scn[:], 0.0)
            g.memset(okt[:], 0.0)
            g.memset(g_tc[:], 0.0)
            g.memset(ckrow[:], 0.0)
            g.iota(out=triu[:], pattern=[[1, 128]], base=1, channel_multiplier=-1,
                   allow_small_or_imprecise_dtypes=True)
            g.iota(out=idf[:], pattern=[[-1, 128]], base=0, channel_multiplier=1,
                   allow_small_or_imprecise_dtypes=True)
            fin(g, "Gc")
        c0_gci = now("Gc")

        @block.vector
        def _(v):
            w(v, ("Gc", c0_gci))
            v.tensor_scalar_max(triu[:], triu[:], 0.0)
            v.drain()
            v.tensor_scalar_min(triu[:], triu[:], 1.0)
            v.drain()
            v.tensor_tensor(out=idf[:], in0=idf[:], in1=idf[:], op=OP.mult)
            v.drain()
            v.tensor_scalar_min(idf[:], idf[:], 1.0)
            v.drain()
            v.tensor_scalar(out=idf[:], in0=idf[:], scalar1=-1.0, scalar2=1.0,
                            op0=OP.mult, op1=OP.add)
            v.drain()
            v.tensor_copy(idb[:], idf[:])
            fin(v, "V")
        c0_gc = now("V")

        @block.gpsimd
        def _(g):
            dma(g, "Gd", neg_dram, negi[0:1, 0:1])
        c0_neg = now("Gd")

        with ExitStack() as sc1:
            e1 = sc1.enter_context

            sc0 = ExitStack()
            stg = sc0.enter_context(nc.sbuf_tensor("stg", [128, 3 * D], F32))

            @block.sync
            def _(sy):
                for ci, src in ((0, wq_in), (1, wk_in), (2, wv_in)):
                    for c in range(8):
                        dma(sy, "L", stg[:, D * ci + 128 * c:D * ci + 128 * c + 128],
                            src[128 * c:128 * c + 128, :])
                dma(sy, "L", row128[0:1, 0:NH], bi_in)
                dma(sy, "L", row128[0:1, 64:64 + NH], bf_in)
                dma(sy, "L", idx_sb[:], idx_in)
                for k in range(KOWN):
                    dma(sy, "L", pm_sb[:, 64 * k:64 * k + 64],
                        pm_in[128 * k:128 * k + 128, :])
                dma(sy, "L", gamma[0:1, :], lnw_in)
            c0_l = now("L")

            for half in range(D // 512):
                @block.tensor
                def _(t, half=half):
                    w(t, ("L", c0_l), ("V", max(c0_gc, now("V"))))
                    t.matmul(TPF[:, 0:512], onesf[0:1, 0:128],
                             gamma[0:1, 512 * half:512 * half + 512],
                             start=True, stop=True)
                    fin(t, "T")
                tg = now("T")

                @block.vector
                def _(v, half=half, tg=tg):
                    w(v, ("T", tg))
                    v.tensor_scalar(out=gamma[:, 512 * half:512 * half + 512],
                                    in0=TPF[:, 0:512], scalar1=1.0, scalar2=None,
                                    op0=OP.add)
                    fin(v, "V")

            @block.tensor
            def _(t):
                w(t, ("L", c0_l), ("V", now("V")))
                t.matmul(TPF[:, 0:NH], onesf[0:1, 0:128], row128[0:1, 0:NH],
                         start=True, stop=True)
                t.matmul(TPF[:, 4:4 + NH], onesf[0:1, 0:128],
                         row128[0:1, 64:64 + NH], start=True, stop=True)
                fin(t, "T")
            tb = now("T")

            @block.vector
            def _(v, tb=tb):
                w(v, ("T", tb))
                v.tensor_copy(biasbc[:], TPF[:, 0:64])
                v.tensor_copy(wq_bf[:], stg[:, 0:D])
                v.tensor_copy(wk_bf[:], stg[:, D:2 * D])
                v.tensor_copy(wv_bf[:], stg[:, 2 * D:3 * D])
                fin(v, "V")
            c0_v = now("V")

            @block.sync
            def _(sy):
                w(sy, ("V", c0_v))
                dma(sy, "L", stg[0:4, 0:3 * D], wi_in)
                dma(sy, "L", stg[4:8, 0:3 * D], wf_in)
            c0_l2 = now("L")

            @block.vector
            def _(v):
                w(v, ("L", c0_l2))
                v.tensor_add(stg[0:8, 0:D], stg[0:8, 0:D], stg[0:8, D:2 * D])
                v.drain()
                v.tensor_add(stg[0:8, 0:D], stg[0:8, 0:D], stg[0:8, 2 * D:3 * D])
                fin(v, "V")
            c0_v2 = now("V")

            for c in range(8):
                @block.tensor
                def _(t, c=c):
                    w(t, ("V", max(c0_v2, now("V"))))
                    t.transpose(TPF[:, 0:8], stg[0:8, 128 * c:128 * c + 128],
                                idf[0:8, 0:8])
                    fin(t, "T")
                tw = now("T")

                @block.vector
                def _(v, c=c, tw=tw):
                    w(v, ("T", tw))
                    v.tensor_copy(wg8[:, 8 * c:8 * c + 8], TPF[:, 0:8])
                    fin(v, "V")
            c0_v3 = now("V")
            sc0.close()

            sqf = e1(nc.sbuf_tensor("sqf", [128, D], F32))
            scln = ExitStack()
            xf = [scln.enter_context(nc.sbuf_tensor(f"xf{i}", [128, D], F32))
                  for i in range(3)]
            xlv = [scln.enter_context(nc.sbuf_tensor(f"xlv{i}", [128, D], BF16))
                   for i in range(3)]

            def ln_stream(src_ap, dst_dram, nrows):
                nt = nrows // 128
                vdone = []
                startV = now("V")
                startA = now("A")
                for i in range(nt):
                    b3 = i % 3

                    @block.sync
                    def _(sy, i=i, b3=b3):
                        if i >= 3:
                            w(sy, ("V", vdone[i - 3]))
                        else:
                            w(sy, ("V", startV), ("A", startA))
                        dma(sy, "L", xf[b3][:], src_ap[128 * i:128 * i + 128, :])
                    lded = now("L")

                    @block.vector
                    def _(v, lded=lded, b3=b3, i=i):
                        w(v, ("L", lded))
                        v.tensor_tensor(out=sqf[:], in0=xf[b3][:], in1=xf[b3][:],
                                        op=OP.mult)
                        v.drain()
                        v.reduce_sum(stA[:, i:i + 1], xf[b3][:], axis=AX.X)
                        v.drain()
                        v.reduce_sum(stA[:, 64 + i:64 + i + 1], sqf[:], axis=AX.X)
                        fin(v, "V")
                    vdone.append(now("V"))

                @block.vector
                def _(v):
                    w(v, ("V", vdone[-1]))
                    v.tensor_scalar(out=stB[:, 0:64], in0=stA[:, 0:64],
                                    scalar1=1.0 / D, scalar2=None, op0=OP.mult)
                    v.drain()
                    v.tensor_scalar(out=stB[:, 64:128], in0=stA[:, 64:128],
                                    scalar1=1.0 / D, scalar2=None, op0=OP.mult)
                    v.drain()
                    v.tensor_tensor(out=stB[:, 128:192], in0=stB[:, 0:64],
                                    in1=stB[:, 0:64], op=OP.mult)
                    v.drain()
                    v.tensor_tensor(out=stB[:, 64:128], in0=stB[:, 64:128],
                                    in1=stB[:, 128:192], op=OP.subtract)
                    fin(v, "V")
                vst = now("V")

                @block.scalar
                def _(s, vst=vst):
                    table(s, SET_SQRT)
                    w(s, ("V", vst), ("Gc", c0_gci))
                    s.activation(out=stB[:, 192:256], in_=stB[:, 64:128],
                                 func=AF.Sqrt, bias=eps5[:])
                    fin(s, "A")
                asq = now("A")

                @block.vector
                def _(v, asq=asq):
                    w(v, ("A", asq))
                    v.reciprocal(stB[:, 256:320], stB[:, 192:256])
                    v.drain()
                    v.tensor_tensor(out=stB[:, 320:384], in0=stB[:, 0:64],
                                    in1=stB[:, 256:320], op=OP.mult)
                    v.drain()
                    v.tensor_scalar(out=stB[:, 320:384], in0=stB[:, 320:384],
                                    scalar1=-1.0, scalar2=None, op0=OP.mult)
                    fin(v, "V")
                vrs = now("V")

                adone = []
                for i in range(nt):
                    b3 = i % 3

                    @block.sync
                    def _(sy, i=i, b3=b3):
                        if i >= 3:
                            w(sy, ("A", adone[i - 3][0]))
                        else:
                            w(sy, ("V", vrs))
                        dma(sy, "L", xf[b3][:], src_ap[128 * i:128 * i + 128, :])
                    lded = now("L")

                    @block.scalar
                    def _(s, lded=lded, b3=b3, i=i):
                        w(s, ("L", lded), ("V", vrs))
                        if i >= 3:
                            w(s, ("L", adone[i - 3][1]))
                        s.activation(out=xlv[b3][:], in_=xf[b3][:],
                                     func=AF.Identity,
                                     bias=stB[:, 320 + i:320 + i + 1],
                                     scale=stB[:, 256 + i:256 + i + 1])
                        fin(s, "A")
                    al = now("A")

                    @block.sync
                    def _(sy, al=al, b3=b3, i=i):
                        w(sy, ("A", al))
                        dma(sy, "L", dst_dram[128 * i:128 * i + 128, :], xlv[b3][:])
                    adone.append((al, now("L")))

            maybe_stop(1)
            ln_stream(xfull, xln_dram, S)
            ln_stream(xown, xlno_dram, S4)
            p1_l = now("L")
            maybe_stop(2)

            xt = [scln.enter_context(nc.sbuf_tensor(f"xt{i}", [128, 512], BF16))
                  for i in range(2)]

            def gates(src_dram, dst_dram, nrows):
                WNl = min(512, nrows)
                mm_done = []
                for wdw in range(nrows // WNl):
                    for c in range(8):
                        b2 = (wdw * 8 + c) % 2

                        @block.scalar
                        def _(s, wdw=wdw, c=c, b2=b2, WNl=WNl):
                            idx = wdw * 8 + c
                            if idx >= 2:
                                w(s, ("T", mm_done[idx - 2]))
                            else:
                                w(s, ("L", p1_l), ("T", now("T")))
                            dmat(s, "Q", xt[b2][0:128, 0:WNl],
                                 src_dram[WNl * wdw:WNl * wdw + WNl,
                                          128 * c:128 * c + 128])
                        qld = now("Q")

                        @block.tensor
                        def _(t, qld=qld, c=c, b2=b2, WNl=WNl):
                            w(t, ("Q", qld), ("V", c0_v3))
                            if c == 0:
                                w(t, ("V", now("V")))
                            t.matmul(GPS[:, 0:WNl], wg8[:, 8 * c:8 * c + 8],
                                     xt[b2][0:128, 0:WNl], start=(c == 0),
                                     stop=(c == 7))
                            fin(t, "T")
                        mm_done.append(now("T"))

                    @block.vector
                    def _(v, wdw=wdw, WNl=WNl, gGd=now("Gd")):
                        w(v, ("T", mm_done[-1]), ("Gd", gGd))
                        v.tensor_copy(ser_st[wdw % 2][0:8, 0:WNl], GPS[:, 0:WNl])
                        fin(v, "V")
                    sev = now("V")

                    @block.gpsimd
                    def _(g, wdw=wdw, WNl=WNl, sev=sev, dst_dram=dst_dram):
                        w(g, ("V", sev))
                        dma(g, "Gd", dst_dram[0:8, WNl * wdw:WNl * wdw + WNl],
                            ser_st[wdw % 2][0:8, 0:WNl])

            gates(xln_dram, ser_dram, S)
            gates(xlno_dram, sero_dram, S4)
            ser_l = 0
            ser_gd = now("Gd")
            scln.close()
            maybe_stop(3)

            sliceT = [e1(nc.sbuf_tensor(f"slT{i}", [128, S], BF16)) for i in range(2)]
            kT_sb = [e1(nc.sbuf_tensor(f"kT{i}", [128, S], BF16)) for i in range(2)]
            V_AR = e1(nc.sbuf_tensor("V_AR", [128, 2 * S], BF16))
            oslT = [e1(nc.sbuf_tensor(f"oslT{i}", [128, S4], BF16)) for i in range(2)]
            qT_sb = [e1(nc.sbuf_tensor(f"qT{i}", [128, S4], BF16)) for i in range(2)]
            kTo_sb = [e1(nc.sbuf_tensor(f"kTo{i}", [128, S4], BF16)) for i in range(2)]
            vto_sb = [e1(nc.sbuf_tensor(f"vto{i}", [128, S4], BF16)) for i in range(2)]
            vo_sb = e1(nc.sbuf_tensor("vo_sb", [128, KOWN * 256], BF16))
            u_all = [e1(nc.sbuf_tensor(f"u{i}", [128, S4], BF16)) for i in range(2)]
            Cs_sb = [e1(nc.sbuf_tensor(f"Cs{i}", [128, 512], BF16)) for i in range(2)]
            Cds = e1(nc.sbuf_tensor("Cds", [128, 128], BF16))
            at_tok = e1(nc.sbuf_tensor("at_tok", [128, KOWN * 256], F32))
            ot_sb = e1(nc.sbuf_tensor("ot_sb", [128, 256], F32))
            hout = e1(nc.sbuf_tensor("hout", [128, 256], BF16))

            for h in range(NH):
                head_bar = (("T", now("T")), ("V", now("V")), ("A", now("A")),
                            ("Gc", now("Gc")), ("Gd", max(ser_gd, now("Gd"))),
                            ("Q", now("Q")), ("L", now("L")))

                @block.gpsimd
                def _(g, h=h, head_bar=head_bar):
                    w(g, *head_bar)
                    dma(g, "Gd", scn[:, 0:C], ser_dram[h:h + 1, :])
                    dma(g, "Gd", scn[:, 64:64 + C], ser_dram[4 + h:5 + h, :])
                    dma(g, "Gd", own[:, L_IG:L_IG + 128], sero_dram[h:h + 1, :])
                    dma(g, "Gd", own[:, L_FG:L_FG + 128], sero_dram[4 + h:5 + h, :])
                gd1 = now("Gd")

                @block.vector
                def _(v, h=h, gd1=gd1):
                    w(v, ("Gd", gd1))
                    v.tensor_scalar(out=scn[:, 0:64], in0=scn[:, 0:64],
                                    scalar1=biasbc[:, h:h + 1], scalar2=None,
                                    op0=OP.add)
                    v.tensor_scalar(out=scn[:, 64:128], in0=scn[:, 64:128],
                                    scalar1=biasbc[:, 4 + h:5 + h], scalar2=None,
                                    op0=OP.add)
                    v.tensor_scalar(out=own[:, L_IG:L_IG + 128],
                                    in0=own[:, L_IG:L_IG + 128],
                                    scalar1=biasbc[0:KOWN, h:h + 1], scalar2=None,
                                    op0=OP.add)
                    v.tensor_scalar(out=own[:, L_FG:L_FG + 128],
                                    in0=own[:, L_FG:L_FG + 128],
                                    scalar1=biasbc[0:KOWN, 4 + h:5 + h],
                                    scalar2=None, op0=OP.add)
                    fin(v, "V")
                vb = now("V")

                @block.scalar
                def _(s, vb=vb):
                    table(s, SET_LNEXP)
                    w(s, ("V", vb))
                    s.activation(out=scn[:, 448:512], in_=scn[:, 64:128],
                                 func=AF.Exp, scale=-1.0)
                    s.activation(out=own[:, L_T1:L_T1 + 128],
                                 in_=own[:, L_FG:L_FG + 128], func=AF.Exp,
                                 scale=-1.0)
                    s.drain()
                    s.activation(out=scn[:, 128:192], in_=scn[:, 448:512],
                                 func=AF.Ln, bias=onev[:])
                    s.activation(out=own[:, L_T2:L_T2 + 128],
                                 in_=own[:, L_T1:L_T1 + 128], func=AF.Ln,
                                 bias=onev[0:KOWN, :])
                    fin(s, "A")
                alf = now("A")

                @block.vector
                def _(v, alf=alf):
                    w(v, ("A", alf))
                    v.tensor_scalar(out=scn[:, 128:192], in0=scn[:, 128:192],
                                    scalar1=-1.0, scalar2=None, op0=OP.mult)
                    v.drain()
                    v.tensor_tensor_scan(out=scn[:, 448:512], data0=onesf[:, 0:64],
                                         data1=scn[:, 128:192], initial=zero1[:],
                                         op0=OP.mult, op1=OP.add)
                    fin(v, "V")
                vcs1 = now("V")

                @block.gpsimd
                def _(g, vcs1=vcs1):
                    w(g, ("V", vcs1))
                    dma(g, "Gd", tot_dram, scn[:, 448 + C - 1:448 + C])
                g1 = now("Gd")

                @block.gpsimd
                def _(g, g1=g1):
                    w(g, ("Gd", g1))
                    dma(g, "Gd", row128[0:1, 0:128], tot_dram)
                g2 = now("Gd")

                @block.vector
                def _(v, g2=g2):
                    w(v, ("Gd", g2))
                    v.tensor_tensor_scan(out=row128[0:1, 128:256],
                                         data0=onesf[0:1, 0:128],
                                         data1=row128[0:1, 0:128],
                                         initial=zero1[0:1, 0:1],
                                         op0=OP.mult, op1=OP.add)
                    fin(v, "V")
                vcs2 = now("V")

                @block.gpsimd
                def _(g, vcs2=vcs2):
                    w(g, ("V", vcs2))
                    dma(g, "Gd", row_dram, row128[0:1, 128:256])
                g3 = now("Gd")

                @block.gpsimd
                def _(g, g3=g3):
                    w(g, ("Gd", g3))
                    dma(g, "Gd", colA[:], row_dram)
                g4 = now("Gd")

                @block.vector
                def _(v, g4=g4):
                    w(v, ("Gd", g4))
                    v.tensor_scalar(out=scn[:, 192:256], in0=scn[:, 448:512],
                                    scalar1=colA[:],
                                    scalar2=scn[:, 448 + C - 1:448 + C],
                                    op0=OP.add, op1=OP.subtract)
                    v.drain()
                    v.tensor_sub(scn[:, 256:320], scn[:, 0:64], scn[:, 192:256])
                    v.drain()
                    v.tensor_tensor_scan(out=scn[:, 320:384], data0=onesf[:, 0:64],
                                         data1=scn[:, 256:320], initial=negi[:],
                                         op0=OP.mult, op1=OP.max)
                    fin(v, "V")
                vm1 = now("V")

                @block.gpsimd
                def _(g, vm1=vm1):
                    w(g, ("V", vm1))
                    dma(g, "Gd", tot_dram, scn[:, 320 + C - 1:320 + C])
                g5 = now("Gd")

                @block.gpsimd
                def _(g, g5=g5):
                    w(g, ("Gd", max(g5, c0_neg)))
                    dma(g, "Gd", row128[0:1, 0:1], neg_dram)
                    dma(g, "Gd", row128[0:1, 1:128], tot_dram[0:127, :])
                g6 = now("Gd")

                @block.vector
                def _(v, g6=g6):
                    w(v, ("Gd", g6))
                    v.tensor_tensor_scan(out=row128[0:1, 128:256],
                                         data0=onesf[0:1, 0:128],
                                         data1=row128[0:1, 0:128],
                                         initial=negi[0:1, 0:1],
                                         op0=OP.mult, op1=OP.max)
                    fin(v, "V")
                vm2 = now("V")

                @block.gpsimd
                def _(g, vm2=vm2):
                    w(g, ("V", vm2))
                    dma(g, "Gd", row_dram, row128[0:1, 128:256])
                g7 = now("Gd")

                @block.gpsimd
                def _(g, g7=g7):
                    w(g, ("Gd", g7))
                    dma(g, "Gd", colB[:], row_dram)
                g8 = now("Gd")

                @block.vector
                def _(v, g8=g8):
                    w(v, ("Gd", g8))
                    v.tensor_scalar(out=scn[:, 384:448], in0=scn[:, 320:384],
                                    scalar1=colB[:], scalar2=None, op0=OP.max)
                    fin(v, "V")
                vM = now("V")

                @block.gpsimd
                def _(g, vM=vM):
                    w(g, ("V", vM))
                    dma(g, "Gd", M_dram[0:1, :], negi[0:1, 0:1])
                    dma(g, "Gd", M_dram[1:S + 1, :], scn[:, 384:384 + C])
                    dma(g, "Gd", CS_dram[0:1, :], zero1[0:1, 0:1])
                    dma(g, "Gd", CS_dram[1:S + 1, :], scn[:, 192:192 + C])
                    dma(g, "Gd", g_dram, scn[:, 256:256 + C])
                gdw = now("Gd")

                @block.gpsimd
                def _(g, gdw=gdw):
                    g.wait_ge(sGd, gdw)
                    _ev(g, "wait", "Gd", gdw)
                    g.indirect_dma_start(
                        out=Mb_sb[:], out_offset=None, in_=M_dram,
                        in_offset=bass.IndirectOffsetOnAxis(ap=idx_sb[:], axis=0),
                    ).then_inc(sGd, 16)
                    cnt["Gd"] += 16
                    _ev(g, "inc", "Gd", 16)
                    g.indirect_dma_start(
                        out=CSb_sb[:], out_offset=None, in_=CS_dram,
                        in_offset=bass.IndirectOffsetOnAxis(ap=idx_sb[:], axis=0),
                    ).then_inc(sGd, 16)
                    cnt["Gd"] += 16
                    _ev(g, "inc", "Gd", 16)
                gbnd = now("Gd")

                @block.sync
                def _(sy, gdw=gdw):
                    w(sy, ("Gd", gdw))
                    dma(sy, "L", gj_sb[:], g_dram)
                gjl = now("L")

                @block.tensor
                def _(t, gjl=gjl):
                    w(t, ("L", gjl), ("V", now("V")))
                    t.transpose(TPF[:, 0:NB], gj_sb[:], idf[0:NB, 0:NB])
                    fin(t, "T")
                tgt = now("T")

                @block.vector
                def _(v, tgt=tgt):
                    w(v, ("T", tgt))
                    v.tensor_copy(g_tc[:, 0:64], TPF[:, 0:64])
                    fin(v, "V")
                vgt = now("V")

                @block.vector
                def _(v, gbnd=gbnd, alf=alf):
                    w(v, ("Gd", gbnd), ("A", alf))
                    v.tensor_scalar(out=own[:, L_LF:L_LF + 128],
                                    in0=own[:, L_T2:L_T2 + 128],
                                    scalar1=-1.0, scalar2=None, op0=OP.mult)
                    v.drain()
                    v.tensor_tensor_scan(out=own[:, L_CS:L_CS + 128],
                                         data0=onesf[0:KOWN, 0:128],
                                         data1=own[:, L_LF:L_LF + 128],
                                         initial=CSb_sb[:], op0=OP.mult, op1=OP.add)
                    v.drain()
                    v.tensor_sub(own[:, L_G:L_G + 128], own[:, L_IG:L_IG + 128],
                                 own[:, L_CS:L_CS + 128])
                    v.drain()
                    v.tensor_tensor_scan(out=own[:, L_M:L_M + 128],
                                         data0=onesf[0:KOWN, 0:128],
                                         data1=own[:, L_G:L_G + 128],
                                         initial=Mb_sb[:], op0=OP.mult, op1=OP.max)
                    v.drain()
                    v.reduce_max(s128[0:KOWN, 0:1], own[:, L_G:L_G + 128], axis=AX.X)
                    fin(v, "V")
                vos1 = now("V")

                @block.gpsimd
                def _(g, vos1=vos1):
                    w(g, ("V", vos1))
                    dma(g, "Gd", ck_dram, s128[0:KOWN, 0:1])
                g9 = now("Gd")

                @block.gpsimd
                def _(g, g9=g9):
                    w(g, ("Gd", g9))
                    dma(g, "Gd", ckcol[:], ck_dram)
                    dma(g, "Gd", ckrow[0:1, 0:KOWN], ck_dram)
                g10 = now("Gd")

                @block.vector
                def _(v, g10=g10):
                    w(v, ("Gd", g10))
                    v.tensor_scalar(out=own[:, L_T1:L_T1 + 128],
                                    in0=own[:, L_M:L_M + 128], scalar1=ckcol[:],
                                    scalar2=None, op0=OP.subtract)
                    v.tensor_add(own[:, L_T2:L_T2 + 128], own[:, L_CS:L_CS + 128],
                                 own[:, L_M:L_M + 128])
                    v.tensor_scalar(out=own[:, L_T3:L_T3 + 128],
                                    in0=own[:, L_G:L_G + 128], scalar1=ckcol[:],
                                    scalar2=None, op0=OP.subtract)
                    fin(v, "V")
                vos2 = now("V")

                @block.scalar
                def _(s, vos2=vos2):
                    w(s, ("V", vos2))
                    s.activation(out=own[:, L_B:L_B + 128],
                                 in_=own[:, L_T1:L_T1 + 128], func=AF.Exp,
                                 scale=-1.0)
                    s.activation(out=own[:, L_ET:L_ET + 128],
                                 in_=own[:, L_T2:L_T2 + 128], func=AF.Exp,
                                 scale=-1.0)
                    s.activation(out=own[:, L_AO:L_AO + 128],
                                 in_=own[:, L_T3:L_T3 + 128], func=AF.Exp)
                    fin(s, "A")
                aoe = now("A")

                @block.tensor
                def _(t, aoe=aoe):
                    w(t, ("A", aoe), ("V", now("V")), ("Gd", now("Gd")))
                    t.transpose(TPF[:, 0:KOWN], own[:, L_AO:L_AO + 128],
                                idf[0:KOWN, 0:KOWN])
                    t.matmul(TPF[:, 64:64 + KOWN], onesf[0:1, 0:128],
                             ckrow[0:1, 0:KOWN], start=True, stop=True)
                    fin(t, "T")
                tao = now("T")

                @block.vector
                def _(v, tao=tao):
                    w(v, ("T", tao))
                    v.tensor_copy(okt[:, 0:64], TPF[:, 0:64])
                    v.tensor_copy(okt[:, 64:128], TPF[:, 64:128])
                    fin(v, "V")
                vao = now("V")

                @block.vector
                def _(v, vao=vao, vgt=vgt):
                    w(v, ("V", max(vao, vgt)), ("A", now("A")))
                    for k in range(KOWN):
                        v.tensor_scalar(out=Ahat[:, 64 * k:64 * k + 64],
                                        in0=g_tc[:, 0:64],
                                        scalar1=okt[:, 64 + k:64 + k + 1],
                                        scalar2=None, op0=OP.subtract)
                    v.drain()
                    v.tensor_add(Ahat[:], Ahat[:], pm_sb[:])
                    fin(v, "V")
                vah = now("V")

                @block.scalar
                def _(s, vah=vah):
                    w(s, ("V", vah))
                    s.activation(out=Ahat[:], in_=Ahat[:], func=AF.Exp)
                    fin(s, "A")
                aah = now("A")

                if h == 0:
                    maybe_stop(4)

                @block.scalar
                def _(s, h=h):
                    w(s, ("L", p1_l), ("T", now("T")), ("Q", now("Q")))
                    for c in range(2):
                        dmat(s, "Q", sliceT[c][:],
                             xln_dram[:, DH * h + 128 * c:DH * h + 128 * c + 128])
                    for c in range(2):
                        dmat(s, "Q", oslT[c][:],
                             xlno_dram[:, DH * h + 128 * c:DH * h + 128 * c + 128])
                qsl = now("Q")

                for c in range(2):
                    for wdw in range(S // 512):
                        @block.tensor
                        def _(t, c=c, wdw=wdw, qsl=qsl, h=h):
                            w(t, ("Q", qsl), ("V", now("V")))
                            t.matmul(CT0[:, 0:512],
                                     wk_bf[:, (2 * h + c) * 128:(2 * h + c) * 128 + 128],
                                     sliceT[c][:, 512 * wdw:512 * wdw + 512],
                                     start=True, stop=True)
                            t.matmul(CT1[:, 0:512],
                                     wv_bf[:, (2 * h + c) * 128:(2 * h + c) * 128 + 128],
                                     sliceT[c][:, 512 * wdw:512 * wdw + 512],
                                     start=True, stop=True)
                            fin(t, "T")
                        tkv = now("T")

                        @block.vector
                        def _(v, c=c, wdw=wdw, tkv=tkv):
                            w(v, ("T", tkv))
                            v.tensor_copy(kT_sb[c][:, 512 * wdw:512 * wdw + 512],
                                          CT0[:, 0:512])
                            v.tensor_copy(
                                V_AR[:, c * S + 512 * wdw:c * S + 512 * wdw + 512],
                                CT1[:, 0:512])
                            fin(v, "V")
                vkv = now("V")

                @block.sync
                def _(sy, vkv=vkv):
                    w(sy, ("V", vkv))
                    dma(sy, "L", vT_dram[0:128, :], V_AR[:, 0:S])
                    dma(sy, "L", vT_dram[128:256, :], V_AR[:, S:2 * S])
                vtl = now("L")

                @block.scalar
                def _(s, vtl=vtl):
                    w(s, ("L", vtl), ("Q", now("Q")))
                    for j in range(NB):
                        dmat(s, "Q", V_AR[:, 256 * j:256 * j + 256],
                             vT_dram[:, 128 * j:128 * j + 128])
                vload = now("Q")

                for c in range(2):
                    for wdw in range(NW4):
                        @block.tensor
                        def _(t, c=c, wdw=wdw, qsl=qsl, h=h):
                            w(t, ("Q", qsl), ("V", now("V")))
                            rhs = oslT[c][:, WN4 * wdw:WN4 * wdw + WN4]
                            base = (2 * h + c) * 128
                            t.matmul(CT0[:, 0:WN4], wq_bf[:, base:base + 128], rhs,
                                     start=True, stop=True)
                            t.matmul(CT1[:, 0:WN4], wk_bf[:, base:base + 128], rhs,
                                     start=True, stop=True)
                            t.matmul(OUT0[:, 0:WN4], wv_bf[:, base:base + 128], rhs,
                                     start=True, stop=True)
                            fin(t, "T")
                        tow = now("T")

                        @block.vector
                        def _(v, c=c, wdw=wdw, tow=tow):
                            w(v, ("T", tow))
                            v.tensor_scalar(out=qT_sb[c][:, WN4 * wdw:WN4 * wdw + WN4],
                                            in0=CT0[:, 0:WN4], scalar1=0.0625,
                                            scalar2=None, op0=OP.mult)
                            v.tensor_copy(kTo_sb[c][:, WN4 * wdw:WN4 * wdw + WN4],
                                          CT1[:, 0:WN4])
                            v.tensor_copy(vto_sb[c][:, WN4 * wdw:WN4 * wdw + WN4],
                                          OUT0[:, 0:WN4])
                            fin(v, "V")
                vop = now("V")

                for k in range(KOWN):
                    for c in range(2):
                        @block.tensor
                        def _(t, k=k, c=c, vop=vop):
                            w(t, ("V", max(vop, now("V"))))
                            t.transpose(TPB[:, 0:128],
                                        vto_sb[c][:, 128 * k:128 * k + 128], idb[:])
                            fin(t, "T")
                        tvo = now("T")

                        @block.vector
                        def _(v, k=k, c=c, tvo=tvo):
                            w(v, ("T", tvo))
                            v.tensor_copy(
                                vo_sb[:, 256 * k + 128 * c:256 * k + 128 * c + 128],
                                TPB[:, 0:128])
                            fin(v, "V")
                vvo = now("V")

                if h == 0:
                    maybe_stop(5)

                for grp in range(NGRP):
                    jcnt = 4 * (G * grp + G - 1) + 3
                    evacs = []
                    cvdone = []
                    for j in range(jcnt):
                        buf = j % 2
                        CT = CT0 if buf == 0 else CT1

                        @block.tensor
                        def _(t, j=j, CT=CT, grp=grp, vload=vload, aah=aah,
                              vvo=vvo):
                            if j >= 2:
                                w(t, ("V", evacs[j - 2]))
                            else:
                                w(t, ("Q", vload), ("A", aah), ("V", vvo))
                            t.matmul(CT[:, 0:GW], kT_sb[0][:, 128 * j:128 * j + 128],
                                     qT_sb[0][:, GW * grp:GW * grp + GW],
                                     start=True, stop=False)
                            t.matmul(CT[:, 0:GW], kT_sb[1][:, 128 * j:128 * j + 128],
                                     qT_sb[1][:, GW * grp:GW * grp + GW],
                                     start=False, stop=True)
                            fin(t, "T")
                        tqk = now("T")

                        @block.vector
                        def _(v, j=j, buf=buf, CT=CT, grp=grp, tqk=tqk):
                            w(v, ("T", tqk))
                            if j >= 2:
                                w(v, ("T", cvdone[j - 2]))
                            for c4 in range(G):
                                k = G * grp + c4
                                v.tensor_scalar(
                                    out=Cs_sb[buf][:, 128 * c4:128 * c4 + 128],
                                    in0=CT[:, 128 * c4:128 * c4 + 128],
                                    scalar1=Ahat[:, 64 * k + j:64 * k + j + 1],
                                    scalar2=None, op0=OP.mult)
                            fin(v, "V")
                        vev = now("V")
                        evacs.append(vev)

                        @block.tensor
                        def _(t, j=j, buf=buf, vev=vev, gV=now("V"), gA=now("A")):
                            w(t, ("V", vev))
                            st0 = (j == 0)
                            if st0:
                                w(t, ("V", gV), ("A", gA))
                            t.matmul(OUT0[:, 0:GW], V_AR[:, 256 * j:256 * j + 128],
                                     Cs_sb[buf][:, 0:GW], start=st0, stop=False)
                            t.matmul(OUT1[:, 0:GW],
                                     V_AR[:, 256 * j + 128:256 * j + 256],
                                     Cs_sb[buf][:, 0:GW], start=st0, stop=False)
                            t.matmul(OSUM[0:1, 0:GW], onesb[:], Cs_sb[buf][:, 0:GW],
                                     start=st0, stop=False)
                            fin(t, "T")
                        cvdone.append(now("T"))

                    for c4 in range(G):
                        k = G * grp + c4

                        @block.tensor
                        def _(t, k=k):
                            w(t, ("V", now("V")))
                            t.matmul(TPF[:, 0:128],
                                     kTo_sb[0][:, 128 * k:128 * k + 128],
                                     qT_sb[0][:, 128 * k:128 * k + 128],
                                     start=True, stop=False)
                            t.matmul(TPF[:, 0:128],
                                     kTo_sb[1][:, 128 * k:128 * k + 128],
                                     qT_sb[1][:, 128 * k:128 * k + 128],
                                     start=False, stop=True)
                            fin(t, "T")
                        tdg = now("T")

                        @block.vector
                        def _(v, k=k, tdg=tdg):
                            w(v, ("T", tdg))
                            v.tensor_scalar(out=ot_sb[:, 0:128], in0=TPF[:, 0:128],
                                            scalar1=okt[:, k:k + 1], scalar2=None,
                                            op0=OP.mult)
                            v.drain()
                            v.tensor_tensor(out=Cds[:], in0=ot_sb[:, 0:128],
                                            in1=triu[:], op=OP.mult)
                            fin(v, "V")
                        vdg = now("V")

                        @block.tensor
                        def _(t, k=k, c4=c4, vdg=vdg):
                            w(t, ("V", vdg))
                            t.matmul(OUT0[:, 128 * c4:128 * c4 + 128],
                                     vo_sb[:, 256 * k:256 * k + 128], Cds[:],
                                     start=False, stop=True)
                            t.matmul(OUT1[:, 128 * c4:128 * c4 + 128],
                                     vo_sb[:, 256 * k + 128:256 * k + 256], Cds[:],
                                     start=False, stop=True)
                            t.matmul(OSUM[0:1, 128 * c4:128 * c4 + 128], onesb[:],
                                     Cds[:], start=False, stop=True)
                            fin(t, "T")
                    tdone = now("T")

                    @block.vector
                    def _(v, grp=grp, tdone=tdone, gGd=now("Gd")):
                        w(v, ("T", tdone), ("Gd", gGd))
                        v.tensor_copy(u_all[0][:, GW * grp:GW * grp + GW],
                                      OUT0[:, 0:GW])
                        v.tensor_copy(u_all[1][:, GW * grp:GW * grp + GW],
                                      OUT1[:, 0:GW])
                        v.tensor_copy(osr[0:1, 0:GW], OSUM[0:1, 0:GW])
                        fin(v, "V")
                    vgp = now("V")

                    @block.gpsimd
                    def _(g, grp=grp, vgp=vgp):
                        w(g, ("V", vgp))
                        dma(g, "Gd", osum_dram[0:1, :], osr[0:1, 0:GW])
                    gos = now("Gd")

                    @block.gpsimd
                    def _(g, grp=grp, gos=gos):
                        w(g, ("Gd", gos))
                        dma(g, "Gd", osum_all[G * grp:G * grp + G, :],
                            osum_dram[0:1, :])
                ua_done = now("V")
                p3gos = now("Gd")

                if h == 0:
                    maybe_stop(6)

                @block.vector
                def _(v, p3gos=p3gos):
                    w(v, ("Gd", p3gos))
                    v.tensor_mul(own[:, L_T1:L_T1 + 128], osum_all[:],
                                 own[:, L_B:L_B + 128])
                    fin(v, "V")
                va1 = now("V")

                @block.scalar
                def _(s, va1=va1):
                    table(s, SET_SQRT)
                    w(s, ("V", va1))
                    s.activation(out=own[:, L_T1:L_T1 + 128],
                                 in_=own[:, L_T1:L_T1 + 128], func=AF.Abs)
                    fin(s, "A")
                aab = now("A")

                @block.vector
                def _(v, aab=aab):
                    w(v, ("A", aab))
                    v.tensor_max(own[:, L_T1:L_T1 + 128], own[:, L_T1:L_T1 + 128],
                                 own[:, L_ET:L_ET + 128])
                    v.drain()
                    v.tensor_scalar(out=own[:, L_T1:L_T1 + 128],
                                    in0=own[:, L_T1:L_T1 + 128],
                                    scalar1=1e-8, scalar2=None, op0=OP.add)
                    v.drain()
                    v.reciprocal(own[:, L_T2:L_T2 + 128], own[:, L_T1:L_T1 + 128])
                    v.drain()
                    v.tensor_mul(own[:, L_T2:L_T2 + 128], own[:, L_T2:L_T2 + 128],
                                 own[:, L_B:L_B + 128])
                    fin(v, "V")
                val = now("V")

                @block.tensor
                def _(t, val=val):
                    w(t, ("V", val))
                    t.transpose(TPF[:, 0:KOWN], own[:, L_T2:L_T2 + 128],
                                idf[0:KOWN, 0:KOWN])
                    fin(t, "T")
                tal = now("T")

                @block.vector
                def _(v, tal=tal):
                    w(v, ("T", tal))
                    v.tensor_copy(okt[:, 128:192], TPF[:, 0:64])
                    fin(v, "V")
                valc = now("V")

                for k in range(KOWN):
                    @block.tensor
                    def _(t, k=k, valc=valc, ua=ua_done):
                        w(t, ("V", max(valc, ua, now("V"))))
                        t.transpose(TPB[:, 0:128],
                                    u_all[0][:, 128 * k:128 * k + 128], idb[:])
                        t.transpose(TPB[:, 128:256],
                                    u_all[1][:, 128 * k:128 * k + 128], idb[:])
                        fin(t, "T")
                    tts = now("T")

                    @block.vector
                    def _(v, k=k, tts=tts):
                        w(v, ("T", tts))
                        v.tensor_scalar(out=at_tok[:, 256 * k:256 * k + 128],
                                        in0=TPB[:, 0:128],
                                        scalar1=okt[:, 128 + k:128 + k + 1],
                                        scalar2=None, op0=OP.mult)
                        v.tensor_scalar(out=at_tok[:, 256 * k + 128:256 * k + 256],
                                        in0=TPB[:, 128:256],
                                        scalar1=okt[:, 128 + k:128 + k + 1],
                                        scalar2=None, op0=OP.mult)
                        v.drain()
                        v.tensor_tensor(out=sqf[:, 0:256],
                                        in0=at_tok[:, 256 * k:256 * k + 256],
                                        in1=at_tok[:, 256 * k:256 * k + 256],
                                        op=OP.mult)
                        v.drain()
                        v.reduce_sum(stA[:, k:k + 1],
                                     at_tok[:, 256 * k:256 * k + 256], axis=AX.X)
                        v.drain()
                        v.reduce_sum(stA[:, 64 + k:64 + k + 1], sqf[:, 0:256],
                                     axis=AX.X)
                        fin(v, "V")
                mhsums = now("V")

                @block.vector
                def _(v, mhsums=mhsums):
                    w(v, ("V", mhsums))
                    v.tensor_scalar(out=stB[:, 0:64], in0=stA[:, 0:64],
                                    scalar1=1.0 / 256, scalar2=None, op0=OP.mult)
                    v.drain()
                    v.tensor_scalar(out=stB[:, 64:128], in0=stA[:, 64:128],
                                    scalar1=1.0 / 256, scalar2=None, op0=OP.mult)
                    v.drain()
                    v.tensor_tensor(out=stB[:, 128:192], in0=stB[:, 0:64],
                                    in1=stB[:, 0:64], op=OP.mult)
                    v.drain()
                    v.tensor_tensor(out=stB[:, 64:128], in0=stB[:, 64:128],
                                    in1=stB[:, 128:192], op=OP.subtract)
                    fin(v, "V")
                mhvar = now("V")

                @block.scalar
                def _(s, mhvar=mhvar):
                    w(s, ("V", mhvar))
                    s.activation(out=stB[:, 192:256], in_=stB[:, 64:128],
                                 func=AF.Sqrt, bias=eps5[:])
                    fin(s, "A")
                mhsq = now("A")

                @block.vector
                def _(v, mhsq=mhsq):
                    w(v, ("A", mhsq))
                    v.reciprocal(stB[:, 256:320], stB[:, 192:256])
                    v.drain()
                    v.tensor_tensor(out=stB[:, 320:384], in0=stB[:, 0:64],
                                    in1=stB[:, 256:320], op=OP.mult)
                    v.drain()
                    v.tensor_scalar(out=stB[:, 320:384], in0=stB[:, 320:384],
                                    scalar1=-1.0, scalar2=None, op0=OP.mult)
                    fin(v, "V")
                mhrs = now("V")

                for k in range(KOWN):
                    @block.scalar
                    def _(s, k=k, mhrs=mhrs):
                        w(s, ("V", max(mhrs, now("V"))))
                        s.activation(out=ot_sb[:],
                                     in_=at_tok[:, 256 * k:256 * k + 256],
                                     func=AF.Identity,
                                     bias=stB[:, 320 + k:320 + k + 1],
                                     scale=stB[:, 256 + k:256 + k + 1])
                        fin(s, "A")
                    amh = now("A")

                    @block.vector
                    def _(v, h=h, k=k, amh=amh, gGd=now("Gd")):
                        w(v, ("A", amh), ("Gd", gGd))
                        v.tensor_mul(hout[:], ot_sb[:], gamma[:, DH * h:DH * h + DH])
                        fin(v, "V")
                    vmh = now("V")

                    @block.gpsimd
                    def _(g, h=h, k=k, vmh=vmh):
                        w(g, ("V", vmh))
                        dma(g, "Gd",
                            attn_dram[128 * k:128 * k + 128, DH * h:DH * h + DH],
                            hout[:])
            p3_Gd = now("Gd")

        maybe_stop(7)
        with ExitStack() as sc3:
            e3 = sc3.enter_context
            xo_b = [e3(nc.sbuf_tensor(f"xo{i}", [128, D], F32)) for i in range(2)]
            at_b = [e3(nc.sbuf_tensor(f"at{i}", [128, D], BF16)) for i in range(2)]
            xm_b = [e3(nc.sbuf_tensor(f"xm{i}", [128, D], F32)) for i in range(2)]
            sq4 = e3(nc.sbuf_tensor("sq4", [128, D], F32))
            xl_b = [e3(nc.sbuf_tensor(f"xlb{i}", [128, D], BF16)) for i in range(2)]
            p4v = []
            startV4 = now("V")
            startA4 = now("A")
            for i in range(KOWN):
                b2 = i % 2

                @block.sync
                def _(sy, i=i, b2=b2):
                    w(sy, ("Gd", p3_Gd))
                    if i >= 2:
                        w(sy, ("L", p4v[i - 2][1]))
                    else:
                        w(sy, ("V", startV4), ("A", startA4))
                    dma(sy, "L", xo_b[b2][:], xown[128 * i:128 * i + 128, :])
                    dma(sy, "L", at_b[b2][:], attn_dram[128 * i:128 * i + 128, :])
                lx = now("L")

                @block.scalar
                def _(s, lx=lx, b2=b2, i=i):
                    w(s, ("L", lx))
                    if i >= 2:
                        w(s, ("V", p4v[i - 2][0]))
                    s.activation(out=xm_b[b2][:], in_=at_b[b2][:], func=AF.Copy)
                    fin(s, "A")
                ac = now("A")

                @block.vector
                def _(v, ac=ac, b2=b2, i=i):
                    w(v, ("A", ac))
                    v.tensor_add(xm_b[b2][:], xm_b[b2][:], xo_b[b2][:])
                    v.drain()
                    v.tensor_tensor(out=sq4[:], in0=xm_b[b2][:], in1=xm_b[b2][:],
                                    op=OP.mult)
                    v.drain()
                    v.reduce_sum(stA[:, i:i + 1], xm_b[b2][:], axis=AX.X)
                    v.drain()
                    v.reduce_sum(stA[:, 64 + i:64 + i + 1], sq4[:], axis=AX.X)
                    fin(v, "V")
                vx = now("V")

                @block.sync
                def _(sy, vx=vx, b2=b2, i=i):
                    w(sy, ("V", vx))
                    dma(sy, "L", xmid_dram[128 * i:128 * i + 128, :], xm_b[b2][:])
                p4v.append((vx, now("L")))
            p4sums = now("V")

            @block.vector
            def _(v, p4sums=p4sums):
                w(v, ("V", p4sums))
                v.tensor_scalar(out=stB[:, 0:64], in0=stA[:, 0:64],
                                scalar1=1.0 / D, scalar2=None, op0=OP.mult)
                v.drain()
                v.tensor_scalar(out=stB[:, 64:128], in0=stA[:, 64:128],
                                scalar1=1.0 / D, scalar2=None, op0=OP.mult)
                v.drain()
                v.tensor_tensor(out=stB[:, 128:192], in0=stB[:, 0:64],
                                in1=stB[:, 0:64], op=OP.mult)
                v.drain()
                v.tensor_tensor(out=stB[:, 64:128], in0=stB[:, 64:128],
                                in1=stB[:, 128:192], op=OP.subtract)
                fin(v, "V")
            p4var = now("V")

            @block.scalar
            def _(s, p4var=p4var):
                w(s, ("V", p4var))
                s.activation(out=stB[:, 192:256], in_=stB[:, 64:128],
                             func=AF.Sqrt, bias=eps5[:])
                fin(s, "A")
            p4sq = now("A")

            @block.vector
            def _(v, p4sq=p4sq):
                w(v, ("A", p4sq))
                v.reciprocal(stB[:, 256:320], stB[:, 192:256])
                v.drain()
                v.tensor_tensor(out=stB[:, 320:384], in0=stB[:, 0:64],
                                in1=stB[:, 256:320], op=OP.mult)
                v.drain()
                v.tensor_scalar(out=stB[:, 320:384], in0=stB[:, 320:384],
                                scalar1=-1.0, scalar2=None, op0=OP.mult)
                fin(v, "V")
            p4rs = now("V")

            adn = []
            for i in range(KOWN):
                b2 = i % 2

                @block.sync
                def _(sy, i=i, b2=b2):
                    if i >= 2:
                        w(sy, ("A", adn[i - 2][0]))
                    else:
                        w(sy, ("V", p4rs))
                    dma(sy, "L", xo_b[b2][:], xmid_dram[128 * i:128 * i + 128, :])
                lded = now("L")

                @block.scalar
                def _(s, lded=lded, b2=b2, i=i):
                    w(s, ("L", lded), ("V", p4rs))
                    if i >= 2:
                        w(s, ("L", adn[i - 2][1]))
                    s.activation(out=xl_b[b2][:], in_=xo_b[b2][:], func=AF.Identity,
                                 bias=stB[:, 320 + i:320 + i + 1],
                                 scale=stB[:, 256 + i:256 + i + 1])
                    fin(s, "A")
                al = now("A")

                @block.sync
                def _(sy, al=al, b2=b2, i=i):
                    w(sy, ("A", al))
                    dma(sy, "L", xmln_dram[128 * i:128 * i + 128, :], xl_b[b2][:])
                adn.append((al, now("L")))
            p4a_l = now("L")

        maybe_stop(8)
        with ExitStack() as sc4:
            e4 = sc4.enter_context
            wg_sb = e4(nc.sbuf_tensor("wg_sb", [128, 8 * HID], BF16))
            wu_sb = e4(nc.sbuf_tensor("wu_sb", [128, 8 * HID], BF16))
            wd_b = [e4(nc.sbuf_tensor(f"wd{i}", [128, D], BF16)) for i in range(2)]
            xmt_sb = e4(nc.sbuf_tensor("xmt_sb", [128, 8 * 512], BF16))
            actT = e4(nc.sbuf_tensor("actT", [128, NHT * SBW], BF16))
            silu2 = [e4(nc.sbuf_tensor(f"silu{i}", [128, SBW], F32)) for i in range(2)]
            dn_b = [e4(nc.sbuf_tensor(f"dn{i}", [128, SBW], F32)) for i in range(2)]
            xms_sb = e4(nc.sbuf_tensor("xms_sb", [128, NS4 * D], F32))
            ost = [e4(nc.sbuf_tensor(f"ost{i}", [128, D], F32)) for i in range(NS4)]

            @block.sync
            def _(sy):
                for c in range(8):
                    dma(sy, "L", wg_sb[:, HID * c:HID * c + HID],
                        wgT_in[128 * c:128 * c + 128, :])
                for c in range(8):
                    dma(sy, "L", wu_sb[:, HID * c:HID * c + HID],
                        wuT_in[128 * c:128 * c + 128, :])
            wgl = now("L")

            for sb in range(NSB):
                @block.scalar
                def _(s, sb=sb):
                    w(s, ("L", p4a_l), ("T", now("T")), ("Q", now("Q")))
                    for c in range(8):
                        dmat(s, "Q", xmt_sb[:, 512 * c:512 * c + SBW],
                             xmln_dram[SBW * sb:SBW * sb + SBW,
                                       128 * c:128 * c + 128])
                xml = now("Q")

                @block.sync
                def _(sy, sb=sb):
                    w(sy, ("V", now("V")))
                    for t4 in range(NS4):
                        dma(sy, "L", xms_sb[:, D * t4:D * t4 + D],
                            xmid_dram[SBW * sb + 128 * t4:SBW * sb + 128 * t4 + 128, :])
                xmsl = now("L")

                p1ev = []
                for ht in range(NHT):
                    @block.tensor
                    def _(t, ht=ht, xml=xml, sb=sb):
                        w(t, ("Q", xml), ("L", wgl))
                        if ht >= 2:
                            w(t, ("V", p1ev[ht - 2]))
                        else:
                            w(t, ("V", now("V")), ("A", now("A")))
                        for c in range(8):
                            t.matmul(CT0[:, 0:SBW],
                                     wg_sb[:, HID * c + 128 * ht:HID * c + 128 * ht + 128],
                                     xmt_sb[:, 512 * c:512 * c + SBW],
                                     start=(c == 0), stop=(c == 7))
                        for c in range(8):
                            t.matmul(CT1[:, 0:SBW],
                                     wu_sb[:, HID * c + 128 * ht:HID * c + 128 * ht + 128],
                                     xmt_sb[:, 512 * c:512 * c + SBW],
                                     start=(c == 0), stop=(c == 7))
                        fin(t, "T")
                    tgu = now("T")

                    @block.scalar
                    def _(s, tgu=tgu, ht=ht):
                        table(s, SET_SILU)
                        w(s, ("T", tgu))
                        s.activation(out=silu2[ht % 2][:], in_=CT0[:, 0:SBW],
                                     func=AF.Silu, scale=inv127)
                        fin(s, "A")
                    asl = now("A")

                    @block.vector
                    def _(v, ht=ht, asl=asl):
                        w(v, ("A", asl))
                        v.tensor_tensor(out=actT[:, SBW * ht:SBW * ht + SBW],
                                        in0=CT1[:, 0:SBW], in1=silu2[ht % 2][:],
                                        op=OP.mult)
                        fin(v, "V")
                    p1ev.append(now("V"))
                p1v = now("V")

                DPS = [OUT0, OUT1, CT0, CT1]
                for half in range(2):
                    wd_done = []
                    for ht in range(NHT):
                        wbuf = ht % 2

                        @block.sync
                        def _(sy, ht=ht, wbuf=wbuf, half=half):
                            if ht >= 2:
                                w(sy, ("T", wd_done[ht - 2]))
                            else:
                                w(sy, ("T", now("T")))
                            dma(sy, "L", wd_b[wbuf][:],
                                wdT_in[128 * ht:128 * ht + 128, :])
                        wdl = now("L")

                        @block.tensor
                        def _(t, ht=ht, wbuf=wbuf, half=half, wdl=wdl, p1v=p1v):
                            w(t, ("L", wdl), ("V", p1v))
                            if ht == 0:
                                w(t, ("V", now("V")), ("A", now("A")))
                            for q in range(4):
                                dc = 4 * half + q
                                t.matmul(DPS[q][:, 0:SBW],
                                         wd_b[wbuf][:, 128 * dc:128 * dc + 128],
                                         actT[:, SBW * ht:SBW * ht + SBW],
                                         start=(ht == 0), stop=(ht == NHT - 1))
                            fin(t, "T")
                        wd_done.append(now("T"))
                    tdn = now("T")

                    for q in range(4):
                        dc = 4 * half + q
                        db = q % 2

                        @block.vector
                        def _(v, q=q, tdn=tdn, db=db):
                            w(v, ("T", max(tdn, now("T"))))
                            v.tensor_copy(dn_b[db][:], DPS[q][:, 0:SBW])
                            fin(v, "V")
                        vdn = now("V")

                        for t4 in range(NS4):
                            @block.tensor
                            def _(t, t4=t4, vdn=vdn, db=db):
                                w(t, ("V", max(vdn, now("V"))))
                                t.transpose(TPF[:, 0:128],
                                            dn_b[db][:, 128 * t4:128 * t4 + 128],
                                            idf[:])
                                fin(t, "T")
                            ttr = now("T")

                            @block.vector
                            def _(v, dc=dc, t4=t4, ttr=ttr, xmsl=xmsl,
                                  gO=now("O")):
                                w(v, ("T", ttr), ("L", xmsl), ("O", gO))
                                v.tensor_scalar(
                                    out=ost[t4][:, 128 * dc:128 * dc + 128],
                                    in0=TPF[:, 0:128], scalar1=inv127 * inv127,
                                    scalar2=None, op0=OP.mult)
                                v.drain()
                                v.tensor_tensor(
                                    out=ost[t4][:, 128 * dc:128 * dc + 128],
                                    in0=ost[t4][:, 128 * dc:128 * dc + 128],
                                    in1=xms_sb[:, D * t4 + 128 * dc:D * t4 + 128 * dc + 128],
                                    op=OP.add)
                                fin(v, "V")
                p2v = now("V")

                @block.sync
                def _(sy, sb=sb, p2v=p2v):
                    w(sy, ("V", p2v))
                    for t4 in range(NS4):
                        sy.dma_start(
                            out=y_out[SBW * sb + 128 * t4:SBW * sb + 128 * t4 + 128, :],
                            in_=ost[t4][:]).then_inc(sO, 16)
                        cnt["O"] += 16
                        _ev(sy, "inc", "O", 16)

            @block.sync
            def _(sy):
                sy.wait_ge(sO, cnt["O"])
                _ev(sy, "wait", "O", cnt["O"])
    except _Stop:
        pass
    return nc


def audit():
    engs = {k: list(v) for k, v in LAST_TRACE.items()}
    pos = {k: 0 for k in engs}
    sem = {}
    progress = True
    while progress:
        progress = False
        for k, ev in engs.items():
            while pos[k] < len(ev):
                kind, key, val = ev[pos[k]]
                if kind == "inc":
                    sem[key] = sem.get(key, 0) + val
                    pos[k] += 1
                    progress = True
                else:
                    if sem.get(key, 0) >= val:
                        pos[k] += 1
                        progress = True
                    else:
                        break
    stuck = {k: ev[pos[k]] for k, ev in engs.items() if pos[k] < len(ev)}
    if stuck:
        print("DEADLOCK: stuck engines:")
        for k, evt in stuck.items():
            print(f"  engine {k}: waiting {evt} at {pos[k]}/{len(engs[k])}")
        return False
    return True


def _expand_bd(wb):
    wb = np.asarray(wb, np.float32)
    nq = wb.shape[0]
    out = np.zeros((nq * 4, 128), np.float32)
    for hh in range(nq):
        off = (4 * hh) % 128
        out[4 * hh:4 * hh + 4, off:off + 4] = wb[hh].T
    return out


def _shard(x, S):
    KOWN = (S // 4) // 128
    in_maps, metas = [], []
    for core in range(8):
        b, st = divmod(core, 4)
        own_rows = np.concatenate(
            [np.arange(128 * (st + 4 * k), 128 * (st + 4 * k) + 128)
             for k in range(KOWN)])
        pm = np.zeros((KOWN * 128, 64), np.float32)
        for k in range(KOWN):
            n_k = st + 4 * k
            pm[128 * k:128 * k + 128, n_k:] = -1e30
        idx = np.array([[128 * (st + 4 * k)] for k in range(KOWN)], np.int32)
        in_maps.append({"xfull": np.ascontiguousarray(x[b]),
                        "xown": np.ascontiguousarray(x[b][own_rows]),
                        "pm": pm, "idx": idx})
        metas.append((b, own_rows))
    return in_maps, metas


def run(x, wq, wk, wv, wi, bi, bf_, wf, lnw, w_gate, w_up, w_down, S, G,
        phases=99):
    xq = _expand_bd(wq)
    xk = _expand_bd(wk)
    xv = _expand_bd(wv)
    wgT = np.ascontiguousarray(np.asarray(w_gate).T.astype(ml_dtypes.bfloat16))
    wuT = np.ascontiguousarray(np.asarray(w_up).T.astype(ml_dtypes.bfloat16))
    wdT = np.ascontiguousarray(np.asarray(w_down).T.astype(ml_dtypes.bfloat16))
    shared = {"wq": xq, "wk": xk, "wv": xv,
              "wi": np.ascontiguousarray(np.asarray(wi, np.float32)),
              "wf": np.ascontiguousarray(np.asarray(wf, np.float32)),
              "bi": np.asarray(bi, np.float32).reshape(1, NH),
              "bf": np.asarray(bf_, np.float32).reshape(1, NH),
              "lnw": np.asarray(lnw, np.float32).reshape(1, D),
              "wgT": wgT, "wuT": wuT, "wdT": wdT}
    in_maps, metas = _shard(np.asarray(x, np.float32), S)
    for m in in_maps:
        m.update(shared)
    nc = build(S=S, G=G, phases=phases)
    res = run_bass_kernel_spmd(nc, in_maps, core_ids=list(range(8)))
    out = np.zeros((2, S, D), np.float32)
    for core in range(8):
        b, own_rows = metas[core]
        out[b][own_rows] = np.asarray(res.results[core]["y"], np.float32)
    return out


def kernel(L0_attn_x, L0_attn_wq, L0_attn_wk, L0_attn_wv, L0_attn_wi, L0_attn_bi,
           L0_attn_wf, L0_attn_bf, L0_attn_ln_w, L0_ffn_w_gate, L0_ffn_w_up,
           L0_ffn_w_down):
    return run(L0_attn_x, L0_attn_wq, L0_attn_wk, L0_attn_wv, L0_attn_wi,
               L0_attn_bi, L0_attn_bf, L0_attn_wf, L0_attn_ln_w,
               L0_ffn_w_gate, L0_ffn_w_up, L0_ffn_w_down, S=8192, G=4)


# version 06 (final)
# speedup vs baseline: 1.0086x; 1.0086x over previous
import numpy as np
import ml_dtypes
import concourse.bass as bass
import concourse.mybir as mybir
from concourse.bass_utils import run_bass_kernel_spmd

F32 = mybir.dt.float32
BF16 = mybir.dt.bfloat16
I32 = mybir.dt.int32
AF = mybir.ActivationFunctionType
OP = mybir.AluOpType
AX = mybir.AxisListType

LAST_TRACE = {}
SET_SQRT = 3
SET_LNEXP = 6
SET_SILU = 18

D = 1024
NH = 4
DH = 256
HID = 3072


class _Stop(Exception):
    pass


def build(S=8192, G=4, phases=99):
    NB = S // 128
    S4 = S // 4
    KOWN = S4 // 128
    C = S // 128
    assert C <= 64 and NB <= 64 and KOWN <= 64
    NGRP = KOWN // G
    GW = G * 128
    SBW = min(256, S4)
    NSB = S4 // SBW
    NHT = HID // 128
    NS4 = SBW // 128
    WN4 = min(512, S4)
    NW4 = S4 // WN4
    inv127 = float(1.0 / 127.0)

    nc = bass.Bass("TRN2", target_bir_lowering=False)

    xfull = nc.dram_tensor("xfull", [S, D], F32, kind="ExternalInput").ap()
    xown = nc.dram_tensor("xown", [S4, D], F32, kind="ExternalInput").ap()
    wq_in = nc.dram_tensor("wq", [D, 128], F32, kind="ExternalInput").ap()
    wk_in = nc.dram_tensor("wk", [D, 128], F32, kind="ExternalInput").ap()
    wv_in = nc.dram_tensor("wv", [D, 128], F32, kind="ExternalInput").ap()
    wi_in = nc.dram_tensor("wi", [NH, 3 * D], F32, kind="ExternalInput").ap()
    wf_in = nc.dram_tensor("wf", [NH, 3 * D], F32, kind="ExternalInput").ap()
    bi_in = nc.dram_tensor("bi", [1, NH], F32, kind="ExternalInput").ap()
    bf_in = nc.dram_tensor("bf", [1, NH], F32, kind="ExternalInput").ap()
    lnw_in = nc.dram_tensor("lnw", [1, D], F32, kind="ExternalInput").ap()
    wgT_in = nc.dram_tensor("wgT", [D, HID], BF16, kind="ExternalInput").ap()
    wuT_in = nc.dram_tensor("wuT", [D, HID], BF16, kind="ExternalInput").ap()
    wdT_in = nc.dram_tensor("wdT", [HID, D], BF16, kind="ExternalInput").ap()
    pm_in = nc.dram_tensor("pm", [KOWN * 128, 64], F32, kind="ExternalInput").ap()
    idx_in = nc.dram_tensor("idx", [KOWN, 1], I32, kind="ExternalInput").ap()
    y_out = nc.dram_tensor("y", [S4, D], F32, kind="ExternalOutput").ap()

    xln_dram = nc.dram_tensor("xln_d", [S, D], BF16, kind="Internal").ap()
    xlno_dram = nc.dram_tensor("xlno_d", [S4, D], BF16, kind="Internal").ap()
    vT_dram = nc.dram_tensor("vT_d", [DH, S], BF16, kind="Internal").ap()
    ser_dram = nc.dram_tensor("ser_d", [8, S], F32, kind="Internal").ap()
    sero_dram = nc.dram_tensor("sero_d", [8, S4], F32, kind="Internal").ap()
    M_dram = nc.dram_tensor("M_d", [S + 1, 1], F32, kind="Internal").ap()
    CS_dram = nc.dram_tensor("CS_d", [S + 1, 1], F32, kind="Internal").ap()
    g_dram = nc.dram_tensor("g_d", [NB, 128], F32, kind="Internal").ap()
    tot_dram = nc.dram_tensor("tot_d", [128, 1], F32, kind="Internal").ap()
    row_dram = nc.dram_tensor("row_d", [128, 1], F32, kind="Internal").ap()
    neg_dram = nc.dram_tensor("neg_d", [1, 1], F32, kind="Internal").ap()
    ck_dram = nc.dram_tensor("ck_d", [KOWN, 1], F32, kind="Internal").ap()
    osum_dram = nc.dram_tensor("osum_d", [1, GW], F32, kind="Internal").ap()
    attn_dram = nc.dram_tensor("attn_d", [S4, D], BF16, kind="Internal").ap()
    xmid_dram = nc.dram_tensor("xmid_d", [S4, D], F32, kind="Internal").ap()
    xmln_dram = nc.dram_tensor("xmln_d", [S4, D], BF16, kind="Internal").ap()

    from contextlib import ExitStack
    cnt = {"L": 0, "Q": 0, "Gd": 0, "T": 0, "V": 0, "A": 0, "Gc": 0, "O": 0}
    tbl = {"cur": None}
    global LAST_TRACE
    LAST_TRACE = {}

    def _ev(eng, kind, key, val):
        LAST_TRACE.setdefault(id(eng), []).append((kind, key, val))

    try:
      with ExitStack() as outer:
        e = outer.enter_context
        sL = e(nc.semaphore("sL"))
        sQ = e(nc.semaphore("sQ"))
        sGd = e(nc.semaphore("sGd"))
        sT = e(nc.semaphore("sT"))
        sV = e(nc.semaphore("sV"))
        sA = e(nc.semaphore("sA"))
        sGc = e(nc.semaphore("sGc"))
        sO = e(nc.semaphore("sO"))
        SEM = {"L": sL, "Q": sQ, "Gd": sGd, "T": sT, "V": sV, "A": sA,
               "Gc": sGc, "O": sO}

        CT0 = e(nc.psum_tensor("CT0", [128, 512], F32))
        CT1 = e(nc.psum_tensor("CT1", [128, 512], F32))
        OUT0 = e(nc.psum_tensor("OUT0", [128, 512], F32))
        OUT1 = e(nc.psum_tensor("OUT1", [128, 512], F32))
        OSUM = e(nc.psum_tensor("OSUM", [1, 512], F32))
        TPB = e(nc.psum_tensor("TPB", [128, 512], BF16))
        TPF = e(nc.psum_tensor("TPF", [128, 512], F32))
        GPS = e(nc.psum_tensor("GPS", [8, 512], F32))

        wq_bf = e(nc.sbuf_tensor("wq_bf", [128, D], BF16))
        wk_bf = e(nc.sbuf_tensor("wk_bf", [128, D], BF16))
        wv_bf = e(nc.sbuf_tensor("wv_bf", [128, D], BF16))
        wg8 = e(nc.sbuf_tensor("wg8", [128, 64], BF16))
        gamma = e(nc.sbuf_tensor("gamma", [128, D], F32))
        idf = e(nc.sbuf_tensor("idf", [128, 128], F32))
        idb = e(nc.sbuf_tensor("idb", [128, 128], BF16))
        triu = e(nc.sbuf_tensor("triu", [128, 128], F32))
        onesb = e(nc.sbuf_tensor("onesb", [128, 1], BF16))
        onesf = e(nc.sbuf_tensor("onesf", [128, 128], F32))
        eps5 = e(nc.sbuf_tensor("eps5", [128, 1], F32))
        onev = e(nc.sbuf_tensor("onev", [128, 1], F32))
        negi = e(nc.sbuf_tensor("negi", [128, 1], F32))
        zero1 = e(nc.sbuf_tensor("zero1", [128, 1], F32))
        biasbc = e(nc.sbuf_tensor("biasbc", [128, 64], F32))

        stA = e(nc.sbuf_tensor("stA", [128, 2 * 64], F32))
        stB = e(nc.sbuf_tensor("stB", [128, 6 * 64], F32))
        ser_st = [e(nc.sbuf_tensor(f"ser_st{i}", [8, 512], F32)) for i in range(2)]
        scn = e(nc.sbuf_tensor("scn", [128, 8 * 64], F32))
        row128 = e(nc.sbuf_tensor("row128", [1, 256], F32))
        colA = e(nc.sbuf_tensor("colA", [128, 1], F32))
        colB = e(nc.sbuf_tensor("colB", [128, 1], F32))
        ckcol = e(nc.sbuf_tensor("ckcol", [KOWN, 1], F32))
        ckrow = e(nc.sbuf_tensor("ckrow", [1, 64], F32))
        gj_sb = e(nc.sbuf_tensor("gj_sb", [NB, 128], F32))
        g_tc = e(nc.sbuf_tensor("g_tc", [128, 64], F32))
        pm_sb = e(nc.sbuf_tensor("pm_sb", [128, KOWN * 64], F32))
        idx_sb = e(nc.sbuf_tensor("idx_sb", [KOWN, 1], I32))
        Mb_sb = e(nc.sbuf_tensor("Mb_sb", [KOWN, 1], F32))
        CSb_sb = e(nc.sbuf_tensor("CSb_sb", [KOWN, 1], F32))
        own = e(nc.sbuf_tensor("own", [KOWN, 12 * 128], F32))
        okt = e(nc.sbuf_tensor("okt", [128, 3 * 64], F32))
        Ahat = e(nc.sbuf_tensor("Ahat", [128, KOWN * 64], F32))
        osum_all = e(nc.sbuf_tensor("osum_all", [KOWN, 128], F32))
        osr = e(nc.sbuf_tensor("osr", [1, 512], F32))
        s128 = e(nc.sbuf_tensor("s128", [128, 8], F32))

        block = e(nc.Block())

        (L_IG, L_FG, L_LF, L_CS, L_G, L_M, L_B, L_ET, L_AO, L_T1, L_T2,
         L_T3) = [128 * i for i in range(12)]

        def table(s, idx):
            if tbl["cur"] != idx:
                s.add_instruction(mybir.InstLoadActFuncSet(
                    name=nc.get_next_instruction_name(),
                    act_func_set_id=idx, ins=[], outs=[]))
                for _ in range(4):
                    s.nop(cycle_cnt=2000)
                s.drain()
                tbl["cur"] = idx

        def mm(t, *a, **kw):
            t.matmul(*a, **kw)

        def tp(t, *a, **kw):
            t.transpose(*a, **kw)

        def fin(eng, key):
            eng.drain(fusable=(key == "T")).then_inc(SEM[key], 1)
            cnt[key] += 1
            _ev(eng, "inc", key, 1)

        def dma(eng, key, out, in_):
            eng.dma_start(out=out, in_=in_).then_inc(SEM[key], 16)
            cnt[key] += 16
            _ev(eng, "inc", key, 16)

        def dmat(eng, key, out, in_):
            eng.dma_start_transpose(out=out, in_=in_).then_inc(SEM[key], 16)
            cnt[key] += 16
            _ev(eng, "inc", key, 16)

        def w(eng, *kv):
            for k, v in kv:
                if v > 0:
                    eng.wait_ge(SEM[k], v)
                    _ev(eng, "wait", k, v)

        def now(key):
            return cnt[key]

        def maybe_stop(n):
            if phases <= n:
                @block.sync
                def _(sy):
                    w(sy, ("V", now("V")), ("A", now("A")), ("T", now("T")),
                      ("Gd", now("Gd")), ("Gc", now("Gc")), ("Q", now("Q")),
                      ("L", now("L")))
                    for i in range(S4 // 128):
                        sy.dma_start(out=y_out[128 * i:128 * i + 128, :],
                                     in_=gamma[:]).then_inc(sO, 16)
                        cnt["O"] += 16
                        _ev(sy, "inc", "O", 16)
                    sy.wait_ge(sO, cnt["O"])
                    _ev(sy, "wait", "O", cnt["O"])
                raise _Stop

        @block.gpsimd
        def _(g):
            g.memset(onesb[:], 1.0)
            g.memset(onesf[:], 1.0)
            g.memset(eps5[:], 1e-5)
            g.memset(onev[:], 1.0)
            g.memset(negi[:], -1e30)
            g.memset(zero1[:], 0.0)
            g.memset(stA[:], 0.0)
            g.memset(stB[:], 0.0)
            g.memset(scn[:], 0.0)
            g.memset(okt[:], 0.0)
            g.memset(g_tc[:], 0.0)
            g.memset(ckrow[:], 0.0)
            g.iota(out=triu[:], pattern=[[1, 128]], base=1, channel_multiplier=-1,
                   allow_small_or_imprecise_dtypes=True)
            g.iota(out=idf[:], pattern=[[-1, 128]], base=0, channel_multiplier=1,
                   allow_small_or_imprecise_dtypes=True)
            fin(g, "Gc")
        c0_gci = now("Gc")

        @block.vector
        def _(v):
            w(v, ("Gc", c0_gci))
            v.tensor_scalar_max(triu[:], triu[:], 0.0)
            v.drain()
            v.tensor_scalar_min(triu[:], triu[:], 1.0)
            v.drain()
            v.tensor_tensor(out=idf[:], in0=idf[:], in1=idf[:], op=OP.mult)
            v.drain()
            v.tensor_scalar_min(idf[:], idf[:], 1.0)
            v.drain()
            v.tensor_scalar(out=idf[:], in0=idf[:], scalar1=-1.0, scalar2=1.0,
                            op0=OP.mult, op1=OP.add)
            v.drain()
            v.tensor_copy(idb[:], idf[:])
            fin(v, "V")
        c0_gc = now("V")

        @block.gpsimd
        def _(g):
            dma(g, "Gd", neg_dram, negi[0:1, 0:1])
        c0_neg = now("Gd")

        with ExitStack() as sc1:
            e1 = sc1.enter_context

            sc0 = ExitStack()
            stg = sc0.enter_context(nc.sbuf_tensor("stg", [128, 3 * D], F32))

            @block.sync
            def _(sy):
                for ci, src in ((0, wq_in), (1, wk_in), (2, wv_in)):
                    for c in range(8):
                        dma(sy, "L", stg[:, D * ci + 128 * c:D * ci + 128 * c + 128],
                            src[128 * c:128 * c + 128, :])
                dma(sy, "L", row128[0:1, 0:NH], bi_in)
                dma(sy, "L", row128[0:1, 64:64 + NH], bf_in)
                dma(sy, "L", idx_sb[:], idx_in)
                for k in range(KOWN):
                    dma(sy, "L", pm_sb[:, 64 * k:64 * k + 64],
                        pm_in[128 * k:128 * k + 128, :])
                dma(sy, "L", gamma[0:1, :], lnw_in)
            c0_l = now("L")

            for half in range(D // 512):
                @block.tensor
                def _(t, half=half):
                    w(t, ("L", c0_l), ("V", max(c0_gc, now("V"))))
                    mm(t, TPF[:, 0:512], onesf[0:1, 0:128],
                             gamma[0:1, 512 * half:512 * half + 512],
                             start=True, stop=True)
                    fin(t, "T")
                tg = now("T")

                @block.vector
                def _(v, half=half, tg=tg):
                    w(v, ("T", tg))
                    v.tensor_scalar(out=gamma[:, 512 * half:512 * half + 512],
                                    in0=TPF[:, 0:512], scalar1=1.0, scalar2=None,
                                    op0=OP.add)
                    fin(v, "V")

            @block.tensor
            def _(t):
                w(t, ("L", c0_l), ("V", now("V")))
                mm(t, TPF[:, 0:NH], onesf[0:1, 0:128], row128[0:1, 0:NH],
                         start=True, stop=True)
                mm(t, TPF[:, 4:4 + NH], onesf[0:1, 0:128],
                         row128[0:1, 64:64 + NH], start=True, stop=True)
                fin(t, "T")
            tb = now("T")

            @block.vector
            def _(v, tb=tb):
                w(v, ("T", tb))
                v.tensor_copy(biasbc[:], TPF[:, 0:64])
                v.tensor_copy(wq_bf[:], stg[:, 0:D])
                v.tensor_copy(wk_bf[:], stg[:, D:2 * D])
                v.tensor_copy(wv_bf[:], stg[:, 2 * D:3 * D])
                fin(v, "V")
            c0_v = now("V")

            @block.sync
            def _(sy):
                w(sy, ("V", c0_v))
                dma(sy, "L", stg[0:4, 0:3 * D], wi_in)
                dma(sy, "L", stg[4:8, 0:3 * D], wf_in)
            c0_l2 = now("L")

            @block.vector
            def _(v):
                w(v, ("L", c0_l2))
                v.tensor_add(stg[0:8, 0:D], stg[0:8, 0:D], stg[0:8, D:2 * D])
                v.drain()
                v.tensor_add(stg[0:8, 0:D], stg[0:8, 0:D], stg[0:8, 2 * D:3 * D])
                fin(v, "V")
            c0_v2 = now("V")

            for c in range(8):
                @block.tensor
                def _(t, c=c):
                    w(t, ("V", max(c0_v2, now("V"))))
                    tp(t, TPF[:, 0:8], stg[0:8, 128 * c:128 * c + 128],
                                idf[0:8, 0:8])
                    fin(t, "T")
                tw = now("T")

                @block.vector
                def _(v, c=c, tw=tw):
                    w(v, ("T", tw))
                    v.tensor_copy(wg8[:, 8 * c:8 * c + 8], TPF[:, 0:8])
                    fin(v, "V")
            c0_v3 = now("V")
            sc0.close()

            sqf = e1(nc.sbuf_tensor("sqf", [128, D], F32))
            scln = ExitStack()
            xf = [scln.enter_context(nc.sbuf_tensor(f"xf{i}", [128, D], F32))
                  for i in range(3)]
            xlv = [scln.enter_context(nc.sbuf_tensor(f"xlv{i}", [128, D], BF16))
                   for i in range(3)]

            def ln_stream(src_ap, dst_dram, nrows):
                nt = nrows // 128
                vdone = []
                startV = now("V")
                startA = now("A")
                for i in range(nt):
                    b3 = i % 3

                    @block.sync
                    def _(sy, i=i, b3=b3):
                        if i >= 3:
                            w(sy, ("V", vdone[i - 3]))
                        else:
                            w(sy, ("V", startV), ("A", startA))
                        dma(sy, "L", xf[b3][:], src_ap[128 * i:128 * i + 128, :])
                    lded = now("L")

                    @block.vector
                    def _(v, lded=lded, b3=b3, i=i):
                        w(v, ("L", lded))
                        v.tensor_tensor(out=sqf[:], in0=xf[b3][:], in1=xf[b3][:],
                                        op=OP.mult)
                        v.drain()
                        v.reduce_sum(stA[:, i:i + 1], xf[b3][:], axis=AX.X)
                        v.drain()
                        v.reduce_sum(stA[:, 64 + i:64 + i + 1], sqf[:], axis=AX.X)
                        fin(v, "V")
                    vdone.append(now("V"))

                @block.vector
                def _(v):
                    w(v, ("V", vdone[-1]))
                    v.tensor_scalar(out=stB[:, 0:64], in0=stA[:, 0:64],
                                    scalar1=1.0 / D, scalar2=None, op0=OP.mult)
                    v.drain()
                    v.tensor_scalar(out=stB[:, 64:128], in0=stA[:, 64:128],
                                    scalar1=1.0 / D, scalar2=None, op0=OP.mult)
                    v.drain()
                    v.tensor_tensor(out=stB[:, 128:192], in0=stB[:, 0:64],
                                    in1=stB[:, 0:64], op=OP.mult)
                    v.drain()
                    v.tensor_tensor(out=stB[:, 64:128], in0=stB[:, 64:128],
                                    in1=stB[:, 128:192], op=OP.subtract)
                    fin(v, "V")
                vst = now("V")

                @block.scalar
                def _(s, vst=vst):
                    table(s, SET_SQRT)
                    w(s, ("V", vst), ("Gc", c0_gci))
                    s.activation(out=stB[:, 192:256], in_=stB[:, 64:128],
                                 func=AF.Sqrt, bias=eps5[:])
                    fin(s, "A")
                asq = now("A")

                @block.vector
                def _(v, asq=asq):
                    w(v, ("A", asq))
                    v.reciprocal(stB[:, 256:320], stB[:, 192:256])
                    v.drain()
                    v.tensor_tensor(out=stB[:, 320:384], in0=stB[:, 0:64],
                                    in1=stB[:, 256:320], op=OP.mult)
                    v.drain()
                    v.tensor_scalar(out=stB[:, 320:384], in0=stB[:, 320:384],
                                    scalar1=-1.0, scalar2=None, op0=OP.mult)
                    fin(v, "V")
                vrs = now("V")

                adone = []
                for i in range(nt):
                    b3 = i % 3

                    @block.sync
                    def _(sy, i=i, b3=b3):
                        if i >= 3:
                            w(sy, ("A", adone[i - 3][0]))
                        else:
                            w(sy, ("V", vrs))
                        dma(sy, "L", xf[b3][:], src_ap[128 * i:128 * i + 128, :])
                    lded = now("L")

                    @block.scalar
                    def _(s, lded=lded, b3=b3, i=i):
                        w(s, ("L", lded), ("V", vrs))
                        if i >= 3:
                            w(s, ("L", adone[i - 3][1]))
                        s.activation(out=xlv[b3][:], in_=xf[b3][:],
                                     func=AF.Identity,
                                     bias=stB[:, 320 + i:320 + i + 1],
                                     scale=stB[:, 256 + i:256 + i + 1])
                        fin(s, "A")
                    al = now("A")

                    @block.sync
                    def _(sy, al=al, b3=b3, i=i):
                        w(sy, ("A", al))
                        dma(sy, "L", dst_dram[128 * i:128 * i + 128, :], xlv[b3][:])
                    adone.append((al, now("L")))

            maybe_stop(1)
            ln_stream(xfull, xln_dram, S)
            ln_stream(xown, xlno_dram, S4)
            p1_l = now("L")
            maybe_stop(2)

            xt = [scln.enter_context(nc.sbuf_tensor(f"xt{i}", [128, 512], BF16))
                  for i in range(2)]

            def gates(src_dram, dst_dram, nrows):
                WNl = min(512, nrows)
                mm_done = []
                for wdw in range(nrows // WNl):
                    for c in range(8):
                        b2 = (wdw * 8 + c) % 2

                        @block.scalar
                        def _(s, wdw=wdw, c=c, b2=b2, WNl=WNl):
                            idx = wdw * 8 + c
                            if idx >= 2:
                                w(s, ("T", mm_done[idx - 2]))
                            else:
                                w(s, ("L", p1_l), ("T", now("T")))
                            dmat(s, "Q", xt[b2][0:128, 0:WNl],
                                 src_dram[WNl * wdw:WNl * wdw + WNl,
                                          128 * c:128 * c + 128])
                        qld = now("Q")

                        @block.tensor
                        def _(t, qld=qld, c=c, b2=b2, WNl=WNl):
                            w(t, ("Q", qld), ("V", c0_v3))
                            if c == 0:
                                w(t, ("V", now("V")))
                            mm(t, GPS[:, 0:WNl], wg8[:, 8 * c:8 * c + 8],
                                     xt[b2][0:128, 0:WNl], start=(c == 0),
                                     stop=(c == 7))
                            fin(t, "T")
                        mm_done.append(now("T"))

                    @block.vector
                    def _(v, wdw=wdw, WNl=WNl, gGd=now("Gd")):
                        w(v, ("T", mm_done[-1]), ("Gd", gGd))
                        v.tensor_copy(ser_st[wdw % 2][0:8, 0:WNl], GPS[:, 0:WNl])
                        fin(v, "V")
                    sev = now("V")

                    @block.gpsimd
                    def _(g, wdw=wdw, WNl=WNl, sev=sev, dst_dram=dst_dram):
                        w(g, ("V", sev))
                        dma(g, "Gd", dst_dram[0:8, WNl * wdw:WNl * wdw + WNl],
                            ser_st[wdw % 2][0:8, 0:WNl])

            gates(xln_dram, ser_dram, S)
            gates(xlno_dram, sero_dram, S4)
            ser_l = 0
            ser_gd = now("Gd")
            scln.close()
            maybe_stop(3)

            sliceT = [e1(nc.sbuf_tensor(f"slT{i}", [128, S], BF16)) for i in range(2)]
            kT_sb = [e1(nc.sbuf_tensor(f"kT{i}", [128, S], BF16)) for i in range(2)]
            V_AR = e1(nc.sbuf_tensor("V_AR", [128, 2 * S], BF16))
            oslT = [e1(nc.sbuf_tensor(f"oslT{i}", [128, S4], BF16)) for i in range(2)]
            qT_sb = [e1(nc.sbuf_tensor(f"qT{i}", [128, S4], BF16)) for i in range(2)]
            kTo_sb = [e1(nc.sbuf_tensor(f"kTo{i}", [128, S4], BF16)) for i in range(2)]
            vto_sb = [e1(nc.sbuf_tensor(f"vto{i}", [128, S4], BF16)) for i in range(2)]
            vo_sb = e1(nc.sbuf_tensor("vo_sb", [128, KOWN * 256], BF16))
            u_all = [e1(nc.sbuf_tensor(f"u{i}", [128, S4], BF16)) for i in range(2)]
            Cs_sb = [e1(nc.sbuf_tensor(f"Cs{i}", [128, 512], BF16)) for i in range(2)]
            Cds = e1(nc.sbuf_tensor("Cds", [128, 128], BF16))
            at_tok = e1(nc.sbuf_tensor("at_tok", [128, KOWN * 256], F32))
            ot_sb = e1(nc.sbuf_tensor("ot_sb", [128, 256], F32))
            hout = e1(nc.sbuf_tensor("hout", [128, 256], BF16))

            for h in range(NH):
                head_bar = (("T", now("T")), ("V", now("V")), ("A", now("A")),
                            ("Gc", now("Gc")), ("Gd", max(ser_gd, now("Gd"))),
                            ("Q", now("Q")), ("L", now("L")))

                @block.gpsimd
                def _(g, h=h, head_bar=head_bar):
                    w(g, *head_bar)
                    dma(g, "Gd", scn[:, 0:C], ser_dram[h:h + 1, :])
                    dma(g, "Gd", scn[:, 64:64 + C], ser_dram[4 + h:5 + h, :])
                    dma(g, "Gd", own[:, L_IG:L_IG + 128], sero_dram[h:h + 1, :])
                    dma(g, "Gd", own[:, L_FG:L_FG + 128], sero_dram[4 + h:5 + h, :])
                gd1 = now("Gd")

                @block.vector
                def _(v, h=h, gd1=gd1):
                    w(v, ("Gd", gd1))
                    v.tensor_scalar(out=scn[:, 0:64], in0=scn[:, 0:64],
                                    scalar1=biasbc[:, h:h + 1], scalar2=None,
                                    op0=OP.add)
                    v.tensor_scalar(out=scn[:, 64:128], in0=scn[:, 64:128],
                                    scalar1=biasbc[:, 4 + h:5 + h], scalar2=None,
                                    op0=OP.add)
                    v.tensor_scalar(out=own[:, L_IG:L_IG + 128],
                                    in0=own[:, L_IG:L_IG + 128],
                                    scalar1=biasbc[0:KOWN, h:h + 1], scalar2=None,
                                    op0=OP.add)
                    v.tensor_scalar(out=own[:, L_FG:L_FG + 128],
                                    in0=own[:, L_FG:L_FG + 128],
                                    scalar1=biasbc[0:KOWN, 4 + h:5 + h],
                                    scalar2=None, op0=OP.add)
                    fin(v, "V")
                vb = now("V")

                @block.scalar
                def _(s, vb=vb):
                    table(s, SET_LNEXP)
                    w(s, ("V", vb))
                    s.activation(out=scn[:, 448:512], in_=scn[:, 64:128],
                                 func=AF.Exp, scale=-1.0)
                    s.activation(out=own[:, L_T1:L_T1 + 128],
                                 in_=own[:, L_FG:L_FG + 128], func=AF.Exp,
                                 scale=-1.0)
                    s.drain()
                    s.activation(out=scn[:, 128:192], in_=scn[:, 448:512],
                                 func=AF.Ln, bias=onev[:])
                    s.activation(out=own[:, L_T2:L_T2 + 128],
                                 in_=own[:, L_T1:L_T1 + 128], func=AF.Ln,
                                 bias=onev[0:KOWN, :])
                    fin(s, "A")
                alf = now("A")

                @block.vector
                def _(v, alf=alf):
                    w(v, ("A", alf))
                    v.tensor_scalar(out=scn[:, 128:192], in0=scn[:, 128:192],
                                    scalar1=-1.0, scalar2=None, op0=OP.mult)
                    v.drain()
                    v.tensor_tensor_scan(out=scn[:, 448:512], data0=onesf[:, 0:64],
                                         data1=scn[:, 128:192], initial=zero1[:],
                                         op0=OP.mult, op1=OP.add)
                    fin(v, "V")
                vcs1 = now("V")

                @block.gpsimd
                def _(g, vcs1=vcs1):
                    w(g, ("V", vcs1))
                    dma(g, "Gd", tot_dram, scn[:, 448 + C - 1:448 + C])
                g1 = now("Gd")

                @block.gpsimd
                def _(g, g1=g1):
                    w(g, ("Gd", g1))
                    dma(g, "Gd", row128[0:1, 0:128], tot_dram)
                g2 = now("Gd")

                @block.vector
                def _(v, g2=g2):
                    w(v, ("Gd", g2))
                    v.tensor_tensor_scan(out=row128[0:1, 128:256],
                                         data0=onesf[0:1, 0:128],
                                         data1=row128[0:1, 0:128],
                                         initial=zero1[0:1, 0:1],
                                         op0=OP.mult, op1=OP.add)
                    fin(v, "V")
                vcs2 = now("V")

                @block.gpsimd
                def _(g, vcs2=vcs2):
                    w(g, ("V", vcs2))
                    dma(g, "Gd", row_dram, row128[0:1, 128:256])
                g3 = now("Gd")

                @block.gpsimd
                def _(g, g3=g3):
                    w(g, ("Gd", g3))
                    dma(g, "Gd", colA[:], row_dram)
                g4 = now("Gd")

                @block.vector
                def _(v, g4=g4):
                    w(v, ("Gd", g4))
                    v.tensor_scalar(out=scn[:, 192:256], in0=scn[:, 448:512],
                                    scalar1=colA[:],
                                    scalar2=scn[:, 448 + C - 1:448 + C],
                                    op0=OP.add, op1=OP.subtract)
                    v.drain()
                    v.tensor_sub(scn[:, 256:320], scn[:, 0:64], scn[:, 192:256])
                    v.drain()
                    v.tensor_tensor_scan(out=scn[:, 320:384], data0=onesf[:, 0:64],
                                         data1=scn[:, 256:320], initial=negi[:],
                                         op0=OP.mult, op1=OP.max)
                    fin(v, "V")
                vm1 = now("V")

                @block.gpsimd
                def _(g, vm1=vm1):
                    w(g, ("V", vm1))
                    dma(g, "Gd", tot_dram, scn[:, 320 + C - 1:320 + C])
                g5 = now("Gd")

                @block.gpsimd
                def _(g, g5=g5):
                    w(g, ("Gd", max(g5, c0_neg)))
                    dma(g, "Gd", row128[0:1, 0:1], neg_dram)
                    dma(g, "Gd", row128[0:1, 1:128], tot_dram[0:127, :])
                g6 = now("Gd")

                @block.vector
                def _(v, g6=g6):
                    w(v, ("Gd", g6))
                    v.tensor_tensor_scan(out=row128[0:1, 128:256],
                                         data0=onesf[0:1, 0:128],
                                         data1=row128[0:1, 0:128],
                                         initial=negi[0:1, 0:1],
                                         op0=OP.mult, op1=OP.max)
                    fin(v, "V")
                vm2 = now("V")

                @block.gpsimd
                def _(g, vm2=vm2):
                    w(g, ("V", vm2))
                    dma(g, "Gd", row_dram, row128[0:1, 128:256])
                g7 = now("Gd")

                @block.gpsimd
                def _(g, g7=g7):
                    w(g, ("Gd", g7))
                    dma(g, "Gd", colB[:], row_dram)
                g8 = now("Gd")

                @block.vector
                def _(v, g8=g8):
                    w(v, ("Gd", g8))
                    v.tensor_scalar(out=scn[:, 384:448], in0=scn[:, 320:384],
                                    scalar1=colB[:], scalar2=None, op0=OP.max)
                    fin(v, "V")
                vM = now("V")

                @block.gpsimd
                def _(g, vM=vM):
                    w(g, ("V", vM))
                    dma(g, "Gd", M_dram[0:1, :], negi[0:1, 0:1])
                    dma(g, "Gd", M_dram[1:S + 1, :], scn[:, 384:384 + C])
                    dma(g, "Gd", CS_dram[0:1, :], zero1[0:1, 0:1])
                    dma(g, "Gd", CS_dram[1:S + 1, :], scn[:, 192:192 + C])
                    dma(g, "Gd", g_dram, scn[:, 256:256 + C])
                gdw = now("Gd")

                @block.gpsimd
                def _(g, gdw=gdw):
                    g.wait_ge(sGd, gdw)
                    _ev(g, "wait", "Gd", gdw)
                    g.indirect_dma_start(
                        out=Mb_sb[:], out_offset=None, in_=M_dram,
                        in_offset=bass.IndirectOffsetOnAxis(ap=idx_sb[:], axis=0),
                    ).then_inc(sGd, 16)
                    cnt["Gd"] += 16
                    _ev(g, "inc", "Gd", 16)
                    g.indirect_dma_start(
                        out=CSb_sb[:], out_offset=None, in_=CS_dram,
                        in_offset=bass.IndirectOffsetOnAxis(ap=idx_sb[:], axis=0),
                    ).then_inc(sGd, 16)
                    cnt["Gd"] += 16
                    _ev(g, "inc", "Gd", 16)
                gbnd = now("Gd")

                @block.sync
                def _(sy, gdw=gdw):
                    w(sy, ("Gd", gdw))
                    dma(sy, "L", gj_sb[:], g_dram)
                gjl = now("L")

                @block.tensor
                def _(t, gjl=gjl):
                    w(t, ("L", gjl), ("V", now("V")))
                    tp(t, TPF[:, 0:NB], gj_sb[:], idf[0:NB, 0:NB])
                    fin(t, "T")
                tgt = now("T")

                @block.vector
                def _(v, tgt=tgt):
                    w(v, ("T", tgt))
                    v.tensor_copy(g_tc[:, 0:64], TPF[:, 0:64])
                    fin(v, "V")
                vgt = now("V")

                @block.vector
                def _(v, gbnd=gbnd, alf=alf):
                    w(v, ("Gd", gbnd), ("A", alf))
                    v.tensor_scalar(out=own[:, L_LF:L_LF + 128],
                                    in0=own[:, L_T2:L_T2 + 128],
                                    scalar1=-1.0, scalar2=None, op0=OP.mult)
                    v.drain()
                    v.tensor_tensor_scan(out=own[:, L_CS:L_CS + 128],
                                         data0=onesf[0:KOWN, 0:128],
                                         data1=own[:, L_LF:L_LF + 128],
                                         initial=CSb_sb[:], op0=OP.mult, op1=OP.add)
                    v.drain()
                    v.tensor_sub(own[:, L_G:L_G + 128], own[:, L_IG:L_IG + 128],
                                 own[:, L_CS:L_CS + 128])
                    v.drain()
                    v.tensor_tensor_scan(out=own[:, L_M:L_M + 128],
                                         data0=onesf[0:KOWN, 0:128],
                                         data1=own[:, L_G:L_G + 128],
                                         initial=Mb_sb[:], op0=OP.mult, op1=OP.max)
                    v.drain()
                    v.reduce_max(s128[0:KOWN, 0:1], own[:, L_G:L_G + 128], axis=AX.X)
                    fin(v, "V")
                vos1 = now("V")

                @block.gpsimd
                def _(g, vos1=vos1):
                    w(g, ("V", vos1))
                    dma(g, "Gd", ck_dram, s128[0:KOWN, 0:1])
                g9 = now("Gd")

                @block.gpsimd
                def _(g, g9=g9):
                    w(g, ("Gd", g9))
                    dma(g, "Gd", ckcol[:], ck_dram)
                    dma(g, "Gd", ckrow[0:1, 0:KOWN], ck_dram)
                g10 = now("Gd")

                @block.vector
                def _(v, g10=g10):
                    w(v, ("Gd", g10))
                    v.tensor_scalar(out=own[:, L_T1:L_T1 + 128],
                                    in0=own[:, L_M:L_M + 128], scalar1=ckcol[:],
                                    scalar2=None, op0=OP.subtract)
                    v.tensor_add(own[:, L_T2:L_T2 + 128], own[:, L_CS:L_CS + 128],
                                 own[:, L_M:L_M + 128])
                    v.tensor_scalar(out=own[:, L_T3:L_T3 + 128],
                                    in0=own[:, L_G:L_G + 128], scalar1=ckcol[:],
                                    scalar2=None, op0=OP.subtract)
                    fin(v, "V")
                vos2 = now("V")

                @block.scalar
                def _(s, vos2=vos2):
                    w(s, ("V", vos2))
                    s.activation(out=own[:, L_B:L_B + 128],
                                 in_=own[:, L_T1:L_T1 + 128], func=AF.Exp,
                                 scale=-1.0)
                    s.activation(out=own[:, L_ET:L_ET + 128],
                                 in_=own[:, L_T2:L_T2 + 128], func=AF.Exp,
                                 scale=-1.0)
                    s.activation(out=own[:, L_AO:L_AO + 128],
                                 in_=own[:, L_T3:L_T3 + 128], func=AF.Exp)
                    fin(s, "A")
                aoe = now("A")

                @block.tensor
                def _(t, aoe=aoe):
                    w(t, ("A", aoe), ("V", now("V")), ("Gd", now("Gd")))
                    tp(t, TPF[:, 0:KOWN], own[:, L_AO:L_AO + 128],
                                idf[0:KOWN, 0:KOWN])
                    mm(t, TPF[:, 64:64 + KOWN], onesf[0:1, 0:128],
                             ckrow[0:1, 0:KOWN], start=True, stop=True)
                    fin(t, "T")
                tao = now("T")

                @block.vector
                def _(v, tao=tao):
                    w(v, ("T", tao))
                    v.tensor_copy(okt[:, 0:64], TPF[:, 0:64])
                    v.tensor_copy(okt[:, 64:128], TPF[:, 64:128])
                    fin(v, "V")
                vao = now("V")

                @block.vector
                def _(v, vao=vao, vgt=vgt):
                    w(v, ("V", max(vao, vgt)), ("A", now("A")))
                    for k in range(KOWN):
                        v.tensor_scalar(out=Ahat[:, 64 * k:64 * k + 64],
                                        in0=g_tc[:, 0:64],
                                        scalar1=okt[:, 64 + k:64 + k + 1],
                                        scalar2=None, op0=OP.subtract)
                    v.drain()
                    v.tensor_add(Ahat[:], Ahat[:], pm_sb[:])
                    fin(v, "V")
                vah = now("V")

                @block.scalar
                def _(s, vah=vah):
                    w(s, ("V", vah))
                    s.activation(out=Ahat[:], in_=Ahat[:], func=AF.Exp)
                    fin(s, "A")
                aah = now("A")

                if h == 0:
                    maybe_stop(4)

                @block.scalar
                def _(s, h=h):
                    w(s, ("L", p1_l), ("T", now("T")), ("Q", now("Q")))
                    for c in range(2):
                        dmat(s, "Q", sliceT[c][:],
                             xln_dram[:, DH * h + 128 * c:DH * h + 128 * c + 128])
                    for c in range(2):
                        dmat(s, "Q", oslT[c][:],
                             xlno_dram[:, DH * h + 128 * c:DH * h + 128 * c + 128])
                qsl = now("Q")

                for c in range(2):
                    for wdw in range(S // 512):
                        @block.tensor
                        def _(t, c=c, wdw=wdw, qsl=qsl, h=h):
                            w(t, ("Q", qsl), ("V", now("V")))
                            mm(t, CT0[:, 0:512],
                                     wk_bf[:, (2 * h + c) * 128:(2 * h + c) * 128 + 128],
                                     sliceT[c][:, 512 * wdw:512 * wdw + 512],
                                     start=True, stop=True)
                            mm(t, CT1[:, 0:512],
                                     wv_bf[:, (2 * h + c) * 128:(2 * h + c) * 128 + 128],
                                     sliceT[c][:, 512 * wdw:512 * wdw + 512],
                                     start=True, stop=True)
                            fin(t, "T")
                        tkv = now("T")

                        @block.vector
                        def _(v, c=c, wdw=wdw, tkv=tkv):
                            w(v, ("T", tkv))
                            v.tensor_copy(kT_sb[c][:, 512 * wdw:512 * wdw + 512],
                                          CT0[:, 0:512])
                            v.tensor_copy(
                                V_AR[:, c * S + 512 * wdw:c * S + 512 * wdw + 512],
                                CT1[:, 0:512])
                            fin(v, "V")
                vkv = now("V")

                @block.sync
                def _(sy, vkv=vkv):
                    w(sy, ("V", vkv))
                    dma(sy, "L", vT_dram[0:128, :], V_AR[:, 0:S])
                    dma(sy, "L", vT_dram[128:256, :], V_AR[:, S:2 * S])
                vtl = now("L")

                @block.scalar
                def _(s, vtl=vtl):
                    w(s, ("L", vtl), ("Q", now("Q")))
                    for j in range(NB):
                        dmat(s, "Q", V_AR[:, 256 * j:256 * j + 256],
                             vT_dram[:, 128 * j:128 * j + 128])
                vload = now("Q")

                for c in range(2):
                    for wdw in range(NW4):
                        @block.tensor
                        def _(t, c=c, wdw=wdw, qsl=qsl, h=h):
                            w(t, ("Q", qsl), ("V", now("V")))
                            rhs = oslT[c][:, WN4 * wdw:WN4 * wdw + WN4]
                            base = (2 * h + c) * 128
                            mm(t, CT0[:, 0:WN4], wq_bf[:, base:base + 128], rhs,
                                     start=True, stop=True)
                            mm(t, CT1[:, 0:WN4], wk_bf[:, base:base + 128], rhs,
                                     start=True, stop=True)
                            mm(t, OUT0[:, 0:WN4], wv_bf[:, base:base + 128], rhs,
                                     start=True, stop=True)
                            fin(t, "T")
                        tow = now("T")

                        @block.vector
                        def _(v, c=c, wdw=wdw, tow=tow):
                            w(v, ("T", tow))
                            v.tensor_scalar(out=qT_sb[c][:, WN4 * wdw:WN4 * wdw + WN4],
                                            in0=CT0[:, 0:WN4], scalar1=0.0625,
                                            scalar2=None, op0=OP.mult)
                            v.tensor_copy(kTo_sb[c][:, WN4 * wdw:WN4 * wdw + WN4],
                                          CT1[:, 0:WN4])
                            v.tensor_copy(vto_sb[c][:, WN4 * wdw:WN4 * wdw + WN4],
                                          OUT0[:, 0:WN4])
                            fin(v, "V")
                vop = now("V")

                for k in range(KOWN):
                    for c in range(2):
                        @block.tensor
                        def _(t, k=k, c=c, vop=vop):
                            w(t, ("V", max(vop, now("V"))))
                            tp(t, TPB[:, 0:128],
                                        vto_sb[c][:, 128 * k:128 * k + 128], idb[:])
                            fin(t, "T")
                        tvo = now("T")

                        @block.vector
                        def _(v, k=k, c=c, tvo=tvo):
                            w(v, ("T", tvo))
                            v.tensor_copy(
                                vo_sb[:, 256 * k + 128 * c:256 * k + 128 * c + 128],
                                TPB[:, 0:128])
                            fin(v, "V")
                vvo = now("V")

                if h == 0:
                    maybe_stop(5)

                for grp in range(NGRP):
                    jcnt = 4 * (G * grp + G - 1) + 3
                    evacs = []
                    cvdone = []
                    for j in range(jcnt):
                        buf = j % 2
                        CT = CT0 if buf == 0 else CT1

                        @block.tensor
                        def _(t, j=j, CT=CT, grp=grp, vload=vload, aah=aah,
                              vvo=vvo):
                            if j >= 2:
                                w(t, ("V", evacs[j - 2]))
                            else:
                                w(t, ("Q", vload), ("A", aah), ("V", vvo))
                            mm(t, CT[:, 0:GW], kT_sb[0][:, 128 * j:128 * j + 128],
                                     qT_sb[0][:, GW * grp:GW * grp + GW],
                                     start=True, stop=False)
                            mm(t, CT[:, 0:GW], kT_sb[1][:, 128 * j:128 * j + 128],
                                     qT_sb[1][:, GW * grp:GW * grp + GW],
                                     start=False, stop=True)
                            fin(t, "T")
                        tqk = now("T")

                        @block.vector
                        def _(v, j=j, buf=buf, CT=CT, grp=grp, tqk=tqk):
                            w(v, ("T", tqk))
                            if j >= 2:
                                w(v, ("T", cvdone[j - 2]))
                            for c4 in range(G):
                                k = G * grp + c4
                                v.tensor_scalar(
                                    out=Cs_sb[buf][:, 128 * c4:128 * c4 + 128],
                                    in0=CT[:, 128 * c4:128 * c4 + 128],
                                    scalar1=Ahat[:, 64 * k + j:64 * k + j + 1],
                                    scalar2=None, op0=OP.mult)
                            fin(v, "V")
                        vev = now("V")
                        evacs.append(vev)

                        @block.tensor
                        def _(t, j=j, buf=buf, vev=vev, gV=now("V"), gA=now("A")):
                            w(t, ("V", vev))
                            st0 = (j == 0)
                            if st0:
                                w(t, ("V", gV), ("A", gA))
                            mm(t, OUT0[:, 0:GW], V_AR[:, 256 * j:256 * j + 128],
                                     Cs_sb[buf][:, 0:GW], start=st0, stop=False)
                            mm(t, OUT1[:, 0:GW],
                                     V_AR[:, 256 * j + 128:256 * j + 256],
                                     Cs_sb[buf][:, 0:GW], start=st0, stop=False)
                            mm(t, OSUM[0:1, 0:GW], onesb[:], Cs_sb[buf][:, 0:GW],
                                     start=st0, stop=False)
                            fin(t, "T")
                        cvdone.append(now("T"))

                    for c4 in range(G):
                        k = G * grp + c4

                        @block.tensor
                        def _(t, k=k):
                            w(t, ("V", now("V")))
                            mm(t, TPF[:, 0:128],
                                     kTo_sb[0][:, 128 * k:128 * k + 128],
                                     qT_sb[0][:, 128 * k:128 * k + 128],
                                     start=True, stop=False)
                            mm(t, TPF[:, 0:128],
                                     kTo_sb[1][:, 128 * k:128 * k + 128],
                                     qT_sb[1][:, 128 * k:128 * k + 128],
                                     start=False, stop=True)
                            fin(t, "T")
                        tdg = now("T")

                        @block.vector
                        def _(v, k=k, tdg=tdg):
                            w(v, ("T", tdg))
                            v.tensor_scalar(out=ot_sb[:, 0:128], in0=TPF[:, 0:128],
                                            scalar1=okt[:, k:k + 1], scalar2=None,
                                            op0=OP.mult)
                            v.drain()
                            v.tensor_tensor(out=Cds[:], in0=ot_sb[:, 0:128],
                                            in1=triu[:], op=OP.mult)
                            fin(v, "V")
                        vdg = now("V")

                        @block.tensor
                        def _(t, k=k, c4=c4, vdg=vdg):
                            w(t, ("V", vdg))
                            mm(t, OUT0[:, 128 * c4:128 * c4 + 128],
                                     vo_sb[:, 256 * k:256 * k + 128], Cds[:],
                                     start=False, stop=True)
                            mm(t, OUT1[:, 128 * c4:128 * c4 + 128],
                                     vo_sb[:, 256 * k + 128:256 * k + 256], Cds[:],
                                     start=False, stop=True)
                            mm(t, OSUM[0:1, 128 * c4:128 * c4 + 128], onesb[:],
                                     Cds[:], start=False, stop=True)
                            fin(t, "T")
                    tdone = now("T")

                    @block.vector
                    def _(v, grp=grp, tdone=tdone, gGd=now("Gd")):
                        w(v, ("T", tdone), ("Gd", gGd))
                        v.tensor_copy(u_all[0][:, GW * grp:GW * grp + GW],
                                      OUT0[:, 0:GW])
                        v.tensor_copy(u_all[1][:, GW * grp:GW * grp + GW],
                                      OUT1[:, 0:GW])
                        v.tensor_copy(osr[0:1, 0:GW], OSUM[0:1, 0:GW])
                        fin(v, "V")
                    vgp = now("V")

                    @block.gpsimd
                    def _(g, grp=grp, vgp=vgp):
                        w(g, ("V", vgp))
                        dma(g, "Gd", osum_dram[0:1, :], osr[0:1, 0:GW])
                    gos = now("Gd")

                    @block.gpsimd
                    def _(g, grp=grp, gos=gos):
                        w(g, ("Gd", gos))
                        dma(g, "Gd", osum_all[G * grp:G * grp + G, :],
                            osum_dram[0:1, :])
                ua_done = now("V")
                p3gos = now("Gd")

                if h == 0:
                    maybe_stop(6)

                @block.vector
                def _(v, p3gos=p3gos):
                    w(v, ("Gd", p3gos))
                    v.tensor_mul(own[:, L_T1:L_T1 + 128], osum_all[:],
                                 own[:, L_B:L_B + 128])
                    fin(v, "V")
                va1 = now("V")

                @block.scalar
                def _(s, va1=va1):
                    table(s, SET_SQRT)
                    w(s, ("V", va1))
                    s.activation(out=own[:, L_T1:L_T1 + 128],
                                 in_=own[:, L_T1:L_T1 + 128], func=AF.Abs)
                    fin(s, "A")
                aab = now("A")

                @block.vector
                def _(v, aab=aab):
                    w(v, ("A", aab))
                    v.tensor_max(own[:, L_T1:L_T1 + 128], own[:, L_T1:L_T1 + 128],
                                 own[:, L_ET:L_ET + 128])
                    v.drain()
                    v.tensor_scalar(out=own[:, L_T1:L_T1 + 128],
                                    in0=own[:, L_T1:L_T1 + 128],
                                    scalar1=1e-8, scalar2=None, op0=OP.add)
                    v.drain()
                    v.reciprocal(own[:, L_T2:L_T2 + 128], own[:, L_T1:L_T1 + 128])
                    v.drain()
                    v.tensor_mul(own[:, L_T2:L_T2 + 128], own[:, L_T2:L_T2 + 128],
                                 own[:, L_B:L_B + 128])
                    fin(v, "V")
                val = now("V")

                @block.tensor
                def _(t, val=val):
                    w(t, ("V", val))
                    tp(t, TPF[:, 0:KOWN], own[:, L_T2:L_T2 + 128],
                                idf[0:KOWN, 0:KOWN])
                    fin(t, "T")
                tal = now("T")

                @block.vector
                def _(v, tal=tal):
                    w(v, ("T", tal))
                    v.tensor_copy(okt[:, 128:192], TPF[:, 0:64])
                    fin(v, "V")
                valc = now("V")

                for k in range(KOWN):
                    @block.tensor
                    def _(t, k=k, valc=valc, ua=ua_done):
                        w(t, ("V", max(valc, ua, now("V"))))
                        tp(t, TPB[:, 0:128],
                                    u_all[0][:, 128 * k:128 * k + 128], idb[:])
                        tp(t, TPB[:, 128:256],
                                    u_all[1][:, 128 * k:128 * k + 128], idb[:])
                        fin(t, "T")
                    tts = now("T")

                    @block.vector
                    def _(v, k=k, tts=tts):
                        w(v, ("T", tts))
                        v.tensor_scalar(out=at_tok[:, 256 * k:256 * k + 128],
                                        in0=TPB[:, 0:128],
                                        scalar1=okt[:, 128 + k:128 + k + 1],
                                        scalar2=None, op0=OP.mult)
                        v.tensor_scalar(out=at_tok[:, 256 * k + 128:256 * k + 256],
                                        in0=TPB[:, 128:256],
                                        scalar1=okt[:, 128 + k:128 + k + 1],
                                        scalar2=None, op0=OP.mult)
                        v.drain()
                        v.tensor_tensor(out=sqf[:, 0:256],
                                        in0=at_tok[:, 256 * k:256 * k + 256],
                                        in1=at_tok[:, 256 * k:256 * k + 256],
                                        op=OP.mult)
                        v.drain()
                        v.reduce_sum(stA[:, k:k + 1],
                                     at_tok[:, 256 * k:256 * k + 256], axis=AX.X)
                        v.drain()
                        v.reduce_sum(stA[:, 64 + k:64 + k + 1], sqf[:, 0:256],
                                     axis=AX.X)
                        fin(v, "V")
                mhsums = now("V")

                @block.vector
                def _(v, mhsums=mhsums):
                    w(v, ("V", mhsums))
                    v.tensor_scalar(out=stB[:, 0:64], in0=stA[:, 0:64],
                                    scalar1=1.0 / 256, scalar2=None, op0=OP.mult)
                    v.drain()
                    v.tensor_scalar(out=stB[:, 64:128], in0=stA[:, 64:128],
                                    scalar1=1.0 / 256, scalar2=None, op0=OP.mult)
                    v.drain()
                    v.tensor_tensor(out=stB[:, 128:192], in0=stB[:, 0:64],
                                    in1=stB[:, 0:64], op=OP.mult)
                    v.drain()
                    v.tensor_tensor(out=stB[:, 64:128], in0=stB[:, 64:128],
                                    in1=stB[:, 128:192], op=OP.subtract)
                    fin(v, "V")
                mhvar = now("V")

                @block.scalar
                def _(s, mhvar=mhvar):
                    w(s, ("V", mhvar))
                    s.activation(out=stB[:, 192:256], in_=stB[:, 64:128],
                                 func=AF.Sqrt, bias=eps5[:])
                    fin(s, "A")
                mhsq = now("A")

                @block.vector
                def _(v, mhsq=mhsq):
                    w(v, ("A", mhsq))
                    v.reciprocal(stB[:, 256:320], stB[:, 192:256])
                    v.drain()
                    v.tensor_tensor(out=stB[:, 320:384], in0=stB[:, 0:64],
                                    in1=stB[:, 256:320], op=OP.mult)
                    v.drain()
                    v.tensor_scalar(out=stB[:, 320:384], in0=stB[:, 320:384],
                                    scalar1=-1.0, scalar2=None, op0=OP.mult)
                    fin(v, "V")
                mhrs = now("V")

                for k in range(KOWN):
                    @block.scalar
                    def _(s, k=k, mhrs=mhrs):
                        w(s, ("V", max(mhrs, now("V"))))
                        s.activation(out=ot_sb[:],
                                     in_=at_tok[:, 256 * k:256 * k + 256],
                                     func=AF.Identity,
                                     bias=stB[:, 320 + k:320 + k + 1],
                                     scale=stB[:, 256 + k:256 + k + 1])
                        fin(s, "A")
                    amh = now("A")

                    @block.vector
                    def _(v, h=h, k=k, amh=amh, gGd=now("Gd")):
                        w(v, ("A", amh), ("Gd", gGd))
                        v.tensor_mul(hout[:], ot_sb[:], gamma[:, DH * h:DH * h + DH])
                        fin(v, "V")
                    vmh = now("V")

                    @block.gpsimd
                    def _(g, h=h, k=k, vmh=vmh):
                        w(g, ("V", vmh))
                        dma(g, "Gd",
                            attn_dram[128 * k:128 * k + 128, DH * h:DH * h + DH],
                            hout[:])
            p3_Gd = now("Gd")

        maybe_stop(7)
        with ExitStack() as sc3:
            e3 = sc3.enter_context
            xo_b = [e3(nc.sbuf_tensor(f"xo{i}", [128, D], F32)) for i in range(2)]
            at_b = [e3(nc.sbuf_tensor(f"at{i}", [128, D], BF16)) for i in range(2)]
            xm_b = [e3(nc.sbuf_tensor(f"xm{i}", [128, D], F32)) for i in range(2)]
            sq4 = e3(nc.sbuf_tensor("sq4", [128, D], F32))
            xl_b = [e3(nc.sbuf_tensor(f"xlb{i}", [128, D], BF16)) for i in range(2)]
            p4v = []
            startV4 = now("V")
            startA4 = now("A")
            for i in range(KOWN):
                b2 = i % 2

                @block.sync
                def _(sy, i=i, b2=b2):
                    w(sy, ("Gd", p3_Gd))
                    if i >= 2:
                        w(sy, ("L", p4v[i - 2][1]))
                    else:
                        w(sy, ("V", startV4), ("A", startA4))
                    dma(sy, "L", xo_b[b2][:], xown[128 * i:128 * i + 128, :])
                    dma(sy, "L", at_b[b2][:], attn_dram[128 * i:128 * i + 128, :])
                lx = now("L")

                @block.scalar
                def _(s, lx=lx, b2=b2, i=i):
                    w(s, ("L", lx))
                    if i >= 2:
                        w(s, ("V", p4v[i - 2][0]))
                    s.activation(out=xm_b[b2][:], in_=at_b[b2][:], func=AF.Copy)
                    fin(s, "A")
                ac = now("A")

                @block.vector
                def _(v, ac=ac, b2=b2, i=i):
                    w(v, ("A", ac))
                    v.tensor_add(xm_b[b2][:], xm_b[b2][:], xo_b[b2][:])
                    v.drain()
                    v.tensor_tensor(out=sq4[:], in0=xm_b[b2][:], in1=xm_b[b2][:],
                                    op=OP.mult)
                    v.drain()
                    v.reduce_sum(stA[:, i:i + 1], xm_b[b2][:], axis=AX.X)
                    v.drain()
                    v.reduce_sum(stA[:, 64 + i:64 + i + 1], sq4[:], axis=AX.X)
                    fin(v, "V")
                vx = now("V")

                @block.sync
                def _(sy, vx=vx, b2=b2, i=i):
                    w(sy, ("V", vx))
                    dma(sy, "L", xmid_dram[128 * i:128 * i + 128, :], xm_b[b2][:])
                p4v.append((vx, now("L")))
            p4sums = now("V")

            @block.vector
            def _(v, p4sums=p4sums):
                w(v, ("V", p4sums))
                v.tensor_scalar(out=stB[:, 0:64], in0=stA[:, 0:64],
                                scalar1=1.0 / D, scalar2=None, op0=OP.mult)
                v.drain()
                v.tensor_scalar(out=stB[:, 64:128], in0=stA[:, 64:128],
                                scalar1=1.0 / D, scalar2=None, op0=OP.mult)
                v.drain()
                v.tensor_tensor(out=stB[:, 128:192], in0=stB[:, 0:64],
                                in1=stB[:, 0:64], op=OP.mult)
                v.drain()
                v.tensor_tensor(out=stB[:, 64:128], in0=stB[:, 64:128],
                                in1=stB[:, 128:192], op=OP.subtract)
                fin(v, "V")
            p4var = now("V")

            @block.scalar
            def _(s, p4var=p4var):
                w(s, ("V", p4var))
                s.activation(out=stB[:, 192:256], in_=stB[:, 64:128],
                             func=AF.Sqrt, bias=eps5[:])
                fin(s, "A")
            p4sq = now("A")

            @block.vector
            def _(v, p4sq=p4sq):
                w(v, ("A", p4sq))
                v.reciprocal(stB[:, 256:320], stB[:, 192:256])
                v.drain()
                v.tensor_tensor(out=stB[:, 320:384], in0=stB[:, 0:64],
                                in1=stB[:, 256:320], op=OP.mult)
                v.drain()
                v.tensor_scalar(out=stB[:, 320:384], in0=stB[:, 320:384],
                                scalar1=-1.0, scalar2=None, op0=OP.mult)
                fin(v, "V")
            p4rs = now("V")

            adn = []
            for i in range(KOWN):
                b2 = i % 2

                @block.sync
                def _(sy, i=i, b2=b2):
                    if i >= 2:
                        w(sy, ("A", adn[i - 2][0]))
                    else:
                        w(sy, ("V", p4rs))
                    dma(sy, "L", xo_b[b2][:], xmid_dram[128 * i:128 * i + 128, :])
                lded = now("L")

                @block.scalar
                def _(s, lded=lded, b2=b2, i=i):
                    w(s, ("L", lded), ("V", p4rs))
                    if i >= 2:
                        w(s, ("L", adn[i - 2][1]))
                    s.activation(out=xl_b[b2][:], in_=xo_b[b2][:], func=AF.Identity,
                                 bias=stB[:, 320 + i:320 + i + 1],
                                 scale=stB[:, 256 + i:256 + i + 1])
                    fin(s, "A")
                al = now("A")

                @block.sync
                def _(sy, al=al, b2=b2, i=i):
                    w(sy, ("A", al))
                    dma(sy, "L", xmln_dram[128 * i:128 * i + 128, :], xl_b[b2][:])
                adn.append((al, now("L")))
            p4a_l = now("L")

        maybe_stop(8)
        with ExitStack() as sc4:
            e4 = sc4.enter_context
            wg_sb = e4(nc.sbuf_tensor("wg_sb", [128, 8 * HID], BF16))
            wu_sb = e4(nc.sbuf_tensor("wu_sb", [128, 8 * HID], BF16))
            wd_b = [e4(nc.sbuf_tensor(f"wd{i}", [128, D], BF16)) for i in range(2)]
            xmt_sb = e4(nc.sbuf_tensor("xmt_sb", [128, 8 * 512], BF16))
            actT = e4(nc.sbuf_tensor("actT", [128, NHT * SBW], BF16))
            silu2 = [e4(nc.sbuf_tensor(f"silu{i}", [128, SBW], F32)) for i in range(2)]
            dn_b = [e4(nc.sbuf_tensor(f"dn{i}", [128, SBW], F32)) for i in range(2)]
            xms_sb = e4(nc.sbuf_tensor("xms_sb", [128, NS4 * D], F32))
            ost = [e4(nc.sbuf_tensor(f"ost{i}", [128, D], F32)) for i in range(NS4)]

            @block.sync
            def _(sy):
                for c in range(8):
                    dma(sy, "L", wg_sb[:, HID * c:HID * c + HID],
                        wgT_in[128 * c:128 * c + 128, :])
                for c in range(8):
                    dma(sy, "L", wu_sb[:, HID * c:HID * c + HID],
                        wuT_in[128 * c:128 * c + 128, :])
            wgl = now("L")

            for sb in range(NSB):
                @block.scalar
                def _(s, sb=sb):
                    w(s, ("L", p4a_l), ("T", now("T")), ("Q", now("Q")))
                    for c in range(8):
                        dmat(s, "Q", xmt_sb[:, 512 * c:512 * c + SBW],
                             xmln_dram[SBW * sb:SBW * sb + SBW,
                                       128 * c:128 * c + 128])
                xml = now("Q")

                @block.sync
                def _(sy, sb=sb):
                    w(sy, ("V", now("V")))
                    for t4 in range(NS4):
                        dma(sy, "L", xms_sb[:, D * t4:D * t4 + D],
                            xmid_dram[SBW * sb + 128 * t4:SBW * sb + 128 * t4 + 128, :])
                xmsl = now("L")

                p1ev = []
                for ht in range(NHT):
                    @block.tensor
                    def _(t, ht=ht, xml=xml, sb=sb):
                        w(t, ("Q", xml), ("L", wgl))
                        if ht >= 2:
                            w(t, ("V", p1ev[ht - 2]))
                        else:
                            w(t, ("V", now("V")), ("A", now("A")))
                        for c in range(8):
                            mm(t, CT0[:, 0:SBW],
                                     wg_sb[:, HID * c + 128 * ht:HID * c + 128 * ht + 128],
                                     xmt_sb[:, 512 * c:512 * c + SBW],
                                     start=(c == 0), stop=(c == 7))
                        for c in range(8):
                            mm(t, CT1[:, 0:SBW],
                                     wu_sb[:, HID * c + 128 * ht:HID * c + 128 * ht + 128],
                                     xmt_sb[:, 512 * c:512 * c + SBW],
                                     start=(c == 0), stop=(c == 7))
                        fin(t, "T")
                    tgu = now("T")

                    @block.scalar
                    def _(s, tgu=tgu, ht=ht):
                        table(s, SET_SILU)
                        w(s, ("T", tgu))
                        s.activation(out=silu2[ht % 2][:], in_=CT0[:, 0:SBW],
                                     func=AF.Silu, scale=inv127)
                        fin(s, "A")
                    asl = now("A")

                    @block.vector
                    def _(v, ht=ht, asl=asl):
                        w(v, ("A", asl))
                        v.tensor_tensor(out=actT[:, SBW * ht:SBW * ht + SBW],
                                        in0=CT1[:, 0:SBW], in1=silu2[ht % 2][:],
                                        op=OP.mult)
                        fin(v, "V")
                    p1ev.append(now("V"))
                p1v = now("V")

                DPS = [OUT0, OUT1, CT0, CT1]
                for half in range(2):
                    wd_done = []
                    for ht in range(NHT):
                        wbuf = ht % 2

                        @block.sync
                        def _(sy, ht=ht, wbuf=wbuf, half=half):
                            if ht >= 2:
                                w(sy, ("T", wd_done[ht - 2]))
                            else:
                                w(sy, ("T", now("T")))
                            dma(sy, "L", wd_b[wbuf][:],
                                wdT_in[128 * ht:128 * ht + 128, :])
                        wdl = now("L")

                        @block.tensor
                        def _(t, ht=ht, wbuf=wbuf, half=half, wdl=wdl, p1v=p1v):
                            w(t, ("L", wdl), ("V", p1v))
                            if ht == 0:
                                w(t, ("V", now("V")), ("A", now("A")))
                            for q in range(4):
                                dc = 4 * half + q
                                mm(t, DPS[q][:, 0:SBW],
                                         wd_b[wbuf][:, 128 * dc:128 * dc + 128],
                                         actT[:, SBW * ht:SBW * ht + SBW],
                                         start=(ht == 0), stop=(ht == NHT - 1))
                            fin(t, "T")
                        wd_done.append(now("T"))
                    tdn = now("T")

                    for q in range(4):
                        dc = 4 * half + q
                        db = q % 2

                        @block.vector
                        def _(v, q=q, tdn=tdn, db=db):
                            w(v, ("T", max(tdn, now("T"))))
                            v.tensor_copy(dn_b[db][:], DPS[q][:, 0:SBW])
                            fin(v, "V")
                        vdn = now("V")

                        for t4 in range(NS4):
                            @block.tensor
                            def _(t, t4=t4, vdn=vdn, db=db):
                                w(t, ("V", max(vdn, now("V"))))
                                tp(t, TPF[:, 0:128],
                                            dn_b[db][:, 128 * t4:128 * t4 + 128],
                                            idf[:])
                                fin(t, "T")
                            ttr = now("T")

                            @block.vector
                            def _(v, dc=dc, t4=t4, ttr=ttr, xmsl=xmsl,
                                  gO=now("O")):
                                w(v, ("T", ttr), ("L", xmsl), ("O", gO))
                                v.tensor_scalar(
                                    out=ost[t4][:, 128 * dc:128 * dc + 128],
                                    in0=TPF[:, 0:128], scalar1=inv127 * inv127,
                                    scalar2=None, op0=OP.mult)
                                v.drain()
                                v.tensor_tensor(
                                    out=ost[t4][:, 128 * dc:128 * dc + 128],
                                    in0=ost[t4][:, 128 * dc:128 * dc + 128],
                                    in1=xms_sb[:, D * t4 + 128 * dc:D * t4 + 128 * dc + 128],
                                    op=OP.add)
                                fin(v, "V")
                p2v = now("V")

                @block.sync
                def _(sy, sb=sb, p2v=p2v):
                    w(sy, ("V", p2v))
                    for t4 in range(NS4):
                        sy.dma_start(
                            out=y_out[SBW * sb + 128 * t4:SBW * sb + 128 * t4 + 128, :],
                            in_=ost[t4][:]).then_inc(sO, 16)
                        cnt["O"] += 16
                        _ev(sy, "inc", "O", 16)

            @block.sync
            def _(sy):
                sy.wait_ge(sO, cnt["O"])
                _ev(sy, "wait", "O", cnt["O"])
    except _Stop:
        pass
    return nc


def audit():
    engs = {k: list(v) for k, v in LAST_TRACE.items()}
    pos = {k: 0 for k in engs}
    sem = {}
    progress = True
    while progress:
        progress = False
        for k, ev in engs.items():
            while pos[k] < len(ev):
                kind, key, val = ev[pos[k]]
                if kind == "inc":
                    sem[key] = sem.get(key, 0) + val
                    pos[k] += 1
                    progress = True
                else:
                    if sem.get(key, 0) >= val:
                        pos[k] += 1
                        progress = True
                    else:
                        break
    stuck = {k: ev[pos[k]] for k, ev in engs.items() if pos[k] < len(ev)}
    if stuck:
        print("DEADLOCK: stuck engines:")
        for k, evt in stuck.items():
            print(f"  engine {k}: waiting {evt} at {pos[k]}/{len(engs[k])}")
        return False
    return True


def _expand_bd(wb):
    wb = np.asarray(wb, np.float32)
    nq = wb.shape[0]
    out = np.zeros((nq * 4, 128), np.float32)
    for hh in range(nq):
        off = (4 * hh) % 128
        out[4 * hh:4 * hh + 4, off:off + 4] = wb[hh].T
    return out


def _shard(x, S):
    KOWN = (S // 4) // 128
    in_maps, metas = [], []
    for core in range(8):
        b, st = divmod(core, 4)
        own_rows = np.concatenate(
            [np.arange(128 * (st + 4 * k), 128 * (st + 4 * k) + 128)
             for k in range(KOWN)])
        pm = np.zeros((KOWN * 128, 64), np.float32)
        for k in range(KOWN):
            n_k = st + 4 * k
            pm[128 * k:128 * k + 128, n_k:] = -1e30
        idx = np.array([[128 * (st + 4 * k)] for k in range(KOWN)], np.int32)
        in_maps.append({"xfull": np.ascontiguousarray(x[b]),
                        "xown": np.ascontiguousarray(x[b][own_rows]),
                        "pm": pm, "idx": idx})
        metas.append((b, own_rows))
    return in_maps, metas


def run(x, wq, wk, wv, wi, bi, bf_, wf, lnw, w_gate, w_up, w_down, S, G,
        phases=99):
    xq = _expand_bd(wq)
    xk = _expand_bd(wk)
    xv = _expand_bd(wv)
    wgT = np.ascontiguousarray(np.asarray(w_gate).T.astype(ml_dtypes.bfloat16))
    wuT = np.ascontiguousarray(np.asarray(w_up).T.astype(ml_dtypes.bfloat16))
    wdT = np.ascontiguousarray(np.asarray(w_down).T.astype(ml_dtypes.bfloat16))
    shared = {"wq": xq, "wk": xk, "wv": xv,
              "wi": np.ascontiguousarray(np.asarray(wi, np.float32)),
              "wf": np.ascontiguousarray(np.asarray(wf, np.float32)),
              "bi": np.asarray(bi, np.float32).reshape(1, NH),
              "bf": np.asarray(bf_, np.float32).reshape(1, NH),
              "lnw": np.asarray(lnw, np.float32).reshape(1, D),
              "wgT": wgT, "wuT": wuT, "wdT": wdT}
    in_maps, metas = _shard(np.asarray(x, np.float32), S)
    for m in in_maps:
        m.update(shared)
    nc = build(S=S, G=G, phases=phases)
    res = run_bass_kernel_spmd(nc, in_maps, core_ids=list(range(8)))
    out = np.zeros((2, S, D), np.float32)
    for core in range(8):
        b, own_rows = metas[core]
        out[b][own_rows] = np.asarray(res.results[core]["y"], np.float32)
    return out


def kernel(L0_attn_x, L0_attn_wq, L0_attn_wk, L0_attn_wv, L0_attn_wi, L0_attn_bi,
           L0_attn_wf, L0_attn_bf, L0_attn_ln_w, L0_ffn_w_gate, L0_ffn_w_up,
           L0_ffn_w_down):
    return run(L0_attn_x, L0_attn_wq, L0_attn_wk, L0_attn_wv, L0_attn_wi,
               L0_attn_bi, L0_attn_bf, L0_attn_wf, L0_attn_ln_w,
               L0_ffn_w_gate, L0_ffn_w_up, L0_ffn_w_down, S=8192, G=4)
